# version 49 (final)
# baseline (speedup 1.0000x reference)
import contextlib
import numpy as np
import ml_dtypes

import concourse.bass as bass
import concourse.mybir as mybir
from concourse.bass_utils import run_bass_kernel_spmd

F32 = mybir.dt.float32
BF16 = mybir.dt.bfloat16
F16 = mybir.dt.float16
AF = mybir.ActivationFunctionType
ALU = mybir.AluOpType

B, S, D, H, DK = 32, 2048, 1024, 16, 64
NCORES = 8
BPC = B // NCORES
TT = S // 128
NCH = D // 128
NHG = H // 2
EPS_LN = 1e-5
EPS_SM = 1e-8
ACT_SET_LN_EXP = 6

ENGINES = ("tensor", "scalar", "vector", "gpsimd", "sync")


class Plan:
    def __init__(self):
        self.items = {e: [] for e in ENGINES}

    def add(self, eng, fn):
        self.items[eng].append(fn)


class KB:
    def __init__(self, nc, ctx):
        self.nc = nc
        self.sems = {}
        self.cnt = {}
        for name in ("sp", "gp", "pe", "act", "dve", "pool", "st", "tp", "ost", "os2", "sums",
                     "rld", "pkl", "pks", "fld", "cs", "tq"):
            self.sems[name] = ctx.enter_context(nc.semaphore("s_" + name))
            self.cnt[name] = 0

    def wait(self, p, eng, sem, val=None):
        v = self.cnt[sem] if val is None else val
        if v > 0:
            p.add(eng, lambda e, s=self.sems[sem], vv=v: e.wait_ge(s, vv))
        return v

    def run(self, p, eng, fn, inc=None, by=1):
        if inc is None:
            p.add(eng, fn)
            return None
        p.add(eng, lambda e, s=self.sems[inc], f=fn, b=by: f(e).then_inc(s, b))
        self.cnt[inc] += by
        return self.cnt[inc]

    def dma(self, p, eng, out, in_, sem):
        return self.run(p, eng, lambda e, o=out, i=in_: e.dma_start(o, i), inc=sem, by=16)

    def dmat(self, p, eng, out, in_, sem):
        return self.run(p, eng, lambda e, o=out, i=in_: e.dma_start_transpose(o, i), inc=sem, by=16)

    def soft_barrier(self, p, engines=ENGINES):
        for eng in engines:
            for sem in self.cnt:
                self.wait(p, eng, sem)

    def flush(self, plan):
        with self.nc.Block() as blk:
            for eng in ENGINES:
                fns = plan.items[eng]
                if fns:
                    getattr(blk, eng)(lambda e, fl=tuple(fns): [f(e) for f in fl])


def rep_ap(t, n_rep, width):
    ap = t[:, :].ap
    return bass.AP(t[:, :].tensor, t[:, :].offset, [ap[0], [0, n_rep], [1, width]])


def strided_ap(t, offset_cols, blk_stride, n_blk, width):
    ap = t[:, :].ap
    return bass.AP(t[:, :].tensor, t[:, :].offset + offset_cols,
                   [ap[0], [blk_stride, n_blk], [1, width]])


def _ln_partA(K, p, ps, resid_in, y, cols, sq_scr, pe_wait):
    mc, ssqc = cols[0], cols[1]
    K.wait(p, "vector", "pe", pe_wait)
    resid_cnt = K.run(p, "vector", lambda e: e.scalar_tensor_tensor(
        y[:, :], ps[:, :], 1.0, resid_in[:, :], ALU.mult, ALU.add), inc="dve")
    K.wait(p, "scalar", "dve", resid_cnt)
    K.run(p, "scalar", lambda e: e.activation(
        sq_scr[:, :], y[:, :], AF.Copy, scale=1.0 / D, accum_out=mc[:, :]))
    sq_cnt = K.run(p, "scalar", lambda e: e.activation(
        sq_scr[:, :], y[:, :], AF.Square, scale=D ** -0.5, accum_out=ssqc[:, :]), inc="act")
    return resid_cnt, sq_cnt


def _ln_partB(K, p, y, cols, sq_cnt, outs, epscol, pool_waits=()):
    mc, ssqc, msqc, varc, rstdc, bcolc = cols
    K.wait(p, "vector", "act", sq_cnt)
    m_cnt = K.run(p, "vector", lambda e: e.tensor_mul(msqc[:, :], mc[:, :], mc[:, :]), inc="dve")
    K.wait(p, "vector", "dve", m_cnt)
    var_cnt = K.run(p, "vector", lambda e: e.tensor_scalar(
        varc[:, :], msqc[:, :], -1.0, ssqc[:, :], ALU.mult, ALU.add), inc="dve")
    K.wait(p, "scalar", "dve", var_cnt)
    ln_cnt = K.run(p, "scalar", lambda e: e.activation(
        varc[:, :], varc[:, :], AF.Ln, bias=epscol[:, :]), inc="act")
    K.wait(p, "scalar", "act", ln_cnt)
    rstd_cnt = K.run(p, "scalar", lambda e: e.activation(
        rstdc[:, :], varc[:, :], AF.Exp, scale=-0.5), inc="act")
    K.wait(p, "vector", "act", rstd_cnt)
    b_cnt = K.run(p, "vector", lambda e: e.scalar_tensor_tensor(
        bcolc[:, :], mc[:, :], -1.0, rstdc[:, :], ALU.mult, ALU.mult), inc="dve")
    K.wait(p, "gpsimd", "act", rstd_cnt)
    K.wait(p, "gpsimd", "dve", b_cnt)
    for sem, val in pool_waits:
        K.wait(p, "gpsimd", sem, val)
    done = None
    for tgt in outs:
        done = K.run(p, "gpsimd", lambda e, t_=tgt: e.tensor_scalar(
            t_, y[:, :], rstdc[:, :], bcolc[:, :], ALU.mult, ALU.add), inc="pool")
    return done


DEBUG_TAPS = False


SERIAL_P34 = False


def build_program():
    nc = bass.Bass("TRN2", target_bir_lowering=False)

    x_in = nc.dram_tensor("x", [BPC * S, D], F32, kind="ExternalInput").ap()
    wqk_in = nc.dram_tensor("wqk", [D + 32, 2 * D], F32, kind="ExternalInput").ap()
    wv_in = nc.dram_tensor("wv", [D + 32, D], F32, kind="ExternalInput").ap()
    wo_in = nc.dram_tensor("wo", [D + 32, D], F32, kind="ExternalInput").ap()
    win_in = nc.dram_tensor("win", [D + 32, 128], F32, kind="ExternalInput").ap()
    wf_in = nc.dram_tensor("wf", [96, D], F32, kind="ExternalInput").ap()
    w1_in = nc.dram_tensor("w1", [128, 8], F32, kind="ExternalInput").ap()
    b1_in = nc.dram_tensor("b1", [128, 8], F32, kind="ExternalInput").ap()
    w2_in = nc.dram_tensor("w2", [128, 8], F32, kind="ExternalInput").ap()
    bin_in = nc.dram_tensor("binv", [64, 1], F32, kind="ExternalInput").ap()
    b2_in = nc.dram_tensor("b2", [128, 1], F32, kind="ExternalInput").ap()
    cos_in = nc.dram_tensor("cosT", [S, 32], F16, kind="ExternalInput").ap()
    sin_in = nc.dram_tensor("sinT", [S, 32], F16, kind="ExternalInput").ap()
    cm_in = nc.dram_tensor("cmask", [128, 128], BF16, kind="ExternalInput").ap()
    out_d = nc.dram_tensor("out", [BPC * S, D], F32, kind="ExternalOutput").ap()

    scratch_kind = "ExternalOutput" if DEBUG_TAPS else "Internal"
    wqk_bf = nc.dram_tensor("wqk_bf", [D + 32, 2 * D], BF16, kind=scratch_kind).ap()
    wv_bf = nc.dram_tensor("wv_bf", [D + 32, D], BF16, kind="Internal").ap()
    wo_bf = nc.dram_tensor("wo_bf", [D + 32, D], BF16, kind="Internal").ap()
    sums_d = nc.dram_tensor("sums_d", [BPC, NHG, 2, S], F32, kind=scratch_kind).ap()
    rrec_d = nc.dram_tensor("rrec_d", [BPC, NHG, 2, S], F32, kind="Internal").ap()
    oT_d = nc.dram_tensor("oT_d", [BPC, NHG, 128, S], BF16, kind=scratch_kind).ap()
    if DEBUG_TAPS:
        dbg_qraw = nc.dram_tensor("dbg_qraw", [128, D], F32, kind="ExternalOutput").ap()
        dbg_rq = nc.dram_tensor("dbg_rq", [128, D], F32, kind="ExternalOutput").ap()
        dbg_qT = nc.dram_tensor("dbg_qT", [128, 256], F32, kind="ExternalOutput").ap()
        dbg_vt = nc.dram_tensor("dbg_vt", [128, H * 65], F32, kind="ExternalOutput").ap()
        dbg_ex = nc.dram_tensor("dbg_ex", [128, 1024], F32, kind="ExternalOutput").ap()
        dbg_stagA = nc.dram_tensor("dbg_stagA", [65, S], F32, kind="ExternalOutput").ap()
        dbg_ln1 = nc.dram_tensor("dbg_ln1", [128, D], F32, kind="ExternalOutput").ap()
        dbg_xT = nc.dram_tensor("dbg_xT", [128, 256], F32, kind="ExternalOutput").ap()
        dbg_osl = nc.dram_tensor("dbg_osl", [128, NCH * 128], F32, kind="ExternalOutput").ap()
        dbg_y = nc.dram_tensor("dbg_y", [128, D], F32, kind="ExternalOutput").ap()
        dbg_stat = nc.dram_tensor("dbg_stat", [4, 128], F32, kind="ExternalOutput").ap()
        dbg_lnT = nc.dram_tensor("dbg_lnT", [128, 512], F32, kind="ExternalOutput").ap()
        dbg_g = nc.dram_tensor("dbg_g", [96, 512], F32, kind="ExternalOutput").ap()
        dbg_s2 = nc.dram_tensor("dbg_s2", [64, 512], F32, kind="ExternalOutput").ap()
        dbg_y2 = nc.dram_tensor("dbg_y2", [128, D], F32, kind="ExternalOutput").ap()
        dbg_h0 = nc.dram_tensor("dbg_h0", [64, 512], F32, kind="ExternalOutput").ap()
        dbg_win = nc.dram_tensor("dbg_win", [128, 512], F32, kind="ExternalOutput").ap()
        dbg_ya = nc.dram_tensor("dbg_ya", [BPC, TT, 128, D], F32, kind="ExternalOutput").ap()
        dbg_osla = nc.dram_tensor("dbg_osla", [BPC, TT, 128, D], BF16, kind="ExternalOutput").ap()

    with contextlib.ExitStack() as ctx:
        K = KB(nc, ctx)
        sb = lambda name, shape, dt: ctx.enter_context(nc.sbuf_tensor(name, shape, dt))

        ones32 = sb("ones32", [32, 512], BF16)
        cmask = sb("cmask_sb", [128, 128], BF16)
        win_sb = sb("win_sb", [128, NCH * 128], BF16)
        winA_sb = sb("winA_sb", [32, 128], BF16)
        wf_sb = sb("wf_sb", [96, D], BF16)
        w1_sb = sb("w1_sb", [128, 8], F32)
        w2_sb = sb("w2_sb", [128, 8], F32)
        b1_sb = sb("b1_sb", [128, 8], F32)
        b2_sb = sb("b2_sb", [128, 1], F32)
        cos_r = [sb(f"cos_r{i}", [128, 32], F16) for i in range(4)]
        sin_r = [sb(f"sin_r{i}", [128, 32], F16) for i in range(4)]
        ssq_r = [sb(f"ssqc{i}", [128, 1], F32) for i in range(2)]
        m_r = [sb(f"mc{i}", [128, 1], F32) for i in range(2)]
        msq_r = [sb(f"msqc{i}", [128, 1], F32) for i in range(2)]
        var_r = [sb(f"varc{i}", [128, 1], F32) for i in range(2)]
        rstd_r = [sb(f"rstdc{i}", [128, 1], F32) for i in range(2)]
        bcol_r = [sb(f"bcolc{i}", [128, 1], F32) for i in range(2)]
        epscol = sb("epscol", [128, 1], F32)
        epssm = sb("epssm", [128, 1], F32)
        statcols = lambda r: (m_r[r], ssq_r[r], msq_r[r], var_r[r], rstd_r[r], bcol_r[r])
        st2 = [[sb(f"st2_{i}_{k}", [128, 1], F32) for i in range(6)] for k in range(2)]
        statcols2 = lambda r: tuple(st2[r])

        p = Plan()
        p.add("scalar", lambda e: e.add_instruction(mybir.InstLoadActFuncSet(
            name=nc.get_next_instruction_name(), act_func_set_id=ACT_SET_LN_EXP,
            ins=[], outs=[])))
        K.run(p, "vector", lambda e: e.memset(ones32[:, :], 0.0))
        K.run(p, "vector", lambda e: e.memset(ones32[0:1, :], 1.0), inc="dve")
        K.run(p, "vector", lambda e: e.memset(epscol[:, :], EPS_LN), inc="dve")
        K.run(p, "vector", lambda e: e.memset(epssm[:, :], EPS_SM), inc="dve")
        K.dma(p, "sync", cmask[:, :], cm_in, "sp")
        K.dma(p, "sync", w1_sb[:, :], w1_in, "sp")
        K.dma(p, "sync", w2_sb[:, :], w2_in, "sp")
        K.dma(p, "sync", b2_sb[:, :], b2_in, "sp")
        with contextlib.ExitStack() as pctx:
            wtmp_f = pctx.enter_context(nc.sbuf_tensor("wtmp_f", [128, 2 * D], F32))
            wtmp_b = pctx.enter_context(nc.sbuf_tensor("wtmp_b", [128, 2 * D], BF16))
            b1tmp = pctx.enter_context(nc.sbuf_tensor("b1tmp", [64, 16], F32))
            btmp = pctx.enter_context(nc.sbuf_tensor("btmp", [64, 1], F32))

            def cast_rows(src, dst, rows, cols):
                r0 = 0
                while r0 < rows:
                    r = min(128, rows - r0)
                    v = K.dma(p, "sync", wtmp_f[0:r, 0:cols], src[r0:r0 + r, :], "sp")
                    K.wait(p, "vector", "sp", v)
                    cv = K.run(p, "vector",
                               lambda e, r_=r, c_=cols: e.tensor_copy(wtmp_b[0:r_, 0:c_], wtmp_f[0:r_, 0:c_]),
                               inc="dve")
                    K.wait(p, "sync", "dve", cv)
                    K.dma(p, "sync", dst[r0:r0 + r, :], wtmp_b[0:r, 0:cols], "sp")
                    r0 += r

            cast_rows(wqk_in, wqk_bf, D + 32, 2 * D)
            cast_rows(wv_in, wv_bf, D + 32, D)
            cast_rows(wo_in, wo_bf, D + 32, D)
            win_src = bass.AP(win_in.tensor, win_in.offset,
                              [[128, 128], [128 * 128, NCH], [1, 128]])
            win_dst = bass.AP(win_sb[:, :].tensor, win_sb[:, :].offset,
                              [win_sb[:, :].ap[0], [128, NCH], [1, 128]])
            K.dma(p, "gpsimd", win_dst, win_src, "gp")
            K.dma(p, "gpsimd", winA_sb[:, :], win_in[D:D + 32, :], "gp")
            K.dma(p, "gpsimd", wf_sb[:, :], wf_in, "gp")
            K.dma(p, "sync", b1_sb[:, :], b1_in, "sp")
            prep_sp_done = K.cnt["sp"]
            prep_gp_done = K.cnt["gp"]
            K.flush(p)
        osl_last_sp = 0
        rload_last_gp = 0
        p = Plan()

        for b in range(BPC):
            row0 = b * S
            with contextlib.ExitStack() as sctx:
                ssb = lambda name, shape, dt: sctx.enter_context(nc.sbuf_tensor(f"{name}_{b}", shape, dt))
                qT = ssb("qT", [128, NCH * S], BF16)
                kT = ssb("kT", [128, NCH * S], BF16)
                vt = [ssb(f"vt{i}", [128, H * 65], BF16) for i in range(TT)]

                with contextlib.ExitStack() as p1ctx:
                    psb = lambda name, shape, dt: p1ctx.enter_context(nc.sbuf_tensor(f"{name}_{b}", shape, dt))
                    xTr = psb("xTr", [128, 4 * NCH * 128], BF16)
                    wqkv_sb = psb("wqkv_sb", [128, NCH * 3 * D], BF16)
                    wqkvA_sb = psb("wqkvA_sb", [32, 3 * D], BF16)
                    xb_r = [psb(f"xb{i}", [128, D], BF16) for i in range(5)]
                    qraw_r = [psb(f"qraw{i}", [128, D], BF16) for i in range(3)]
                    kraw_r = [psb(f"kraw{i}", [128, D], BF16) for i in range(3)]
                    rt = [psb(f"rt{i}", [128, 512], BF16) for i in range(4)]
                    rtk = [psb(f"rtk{i}", [128, 512], BF16) for i in range(4)]
                    rq_r = [psb(f"rq{i}", [128, D], BF16) for i in range(3)]
                    rk_r = [psb(f"rk{i}", [128, D], BF16) for i in range(3)]
                    psQ = p1ctx.enter_context(nc.psum_tensor(f"psQ_{b}", [128, 1024], F32))
                    psK = p1ctx.enter_context(nc.psum_tensor(f"psK_{b}", [128, 1024], F32))
                    psV = p1ctx.enter_context(nc.psum_tensor(f"psV_{b}", [128, 1024], F32))

                    K.soft_barrier(p)
                    if b == 0:
                        K.wait(p, "sync", "sp", prep_sp_done)
                        K.wait(p, "sync", "gp", prep_gp_done)
                        K.wait(p, "scalar", "sp", prep_sp_done)
                    K.dma(p, "sync", wqkvA_sb[:, 0:2 * D], wqk_bf[D:D + 32, :], "sp")
                    wldA = K.dma(p, "sync", wqkvA_sb[:, 2 * D:3 * D], wv_bf[D:D + 32, :], "sp")
                    wld_qk = [0] * NCH
                    wld_v = [0] * NCH
                    for c in range(NCH):
                        wld_qk[c] = K.dma(p, "sync", wqkv_sb[:, c * 3 * D:c * 3 * D + 2 * D],
                                          wqk_bf[c * 128:(c + 1) * 128, :], "sp")
                        wld_v[c] = K.dma(p, "scalar", wqkv_sb[:, c * 3 * D + 2 * D:(c + 1) * 3 * D],
                                         wv_bf[c * 128:(c + 1) * 128, :], "fld")
                    for t in range(TT):
                        K.run(p, "vector", lambda e, t_=t: e.memset(
                            strided_ap(vt[t_], 64, 65, H, 1), 1.0),
                            inc="dve" if t == TT - 1 else None)
                    vt_ms = K.cnt["dve"]

                    x_tp = [0] * TT
                    qk_tp = [0] * TT
                    qt_tp = [0] * TT
                    mulk_done = [0] * TT
                    ev_q = [0] * TT
                    ev_k = [0] * TT
                    vev = [0] * TT
                    tile_pe = [0] * TT
                    mul_done = [0] * TT
                    mul_done_gp = [0] * TT
                    rope_done = [0] * TT
                    cast_hist = [0] * TT

                    def emit_xload(tl):
                        if tl >= 5:
                            K.wait(p, "gpsimd", "tp", x_tp[tl - 5])
                        cast_hist[tl] = K.dma(p, "gpsimd", xb_r[tl % 5][:, :],
                                              x_in[row0 + tl * 128: row0 + (tl + 1) * 128, :], "gp")

                    def emit_xT(tx):
                        K.wait(p, "sync", "gp", cast_hist[tx])
                        if tx >= 4:
                            K.wait(p, "sync", "pe", tile_pe[tx - 4])
                        xT_view = bass.AP(xTr[:, :].tensor, xTr[:, :].offset + (tx % 4) * 1024,
                                          [xTr[:, :].ap[0], [128, NCH], [1, 128]])
                        K.dmat(p, "sync", xT_view, xb_r[tx % 5][:, :], "tp")
                        x_tp[tx] = K.cnt["tp"]

                    def emit_kT(tk):
                        K.wait(p, "sync", "dve", mulk_done[tk])
                        kT_view = bass.AP(kT[:, :].tensor, kT[:, :].offset + tk * 128,
                                          [kT[:, :].ap[0], [S, NCH], [1, 128]])
                        K.dmat(p, "sync", kT_view, rk_r[tk % 3][:, :], "tp")
                        qk_tp[tk] = K.cnt["tp"]

                    for tl in range(4):
                        emit_xload(tl)
                    emit_xT(0)
                    emit_xT(1)
                    for t in range(TT):
                        r = t % 3
                        u = t % 4
                        if t + 4 < TT:
                            emit_xload(t + 4)
                        if t + 2 < TT:
                            emit_xT(t + 2)
                        if t >= 1:
                            emit_kT(t - 1)
                        if t >= 4:
                            K.wait(p, "gpsimd", "dve", mul_done[t - 4])
                            K.wait(p, "gpsimd", "pool", mul_done_gp[t - 4])
                        K.dma(p, "gpsimd", cos_r[t % 4][:, :], cos_in[t * 128:(t + 1) * 128, :], "cs")
                        cs_v = K.dma(p, "gpsimd", sin_r[t % 4][:, :], sin_in[t * 128:(t + 1) * 128, :], "cs")

                        K.wait(p, "tensor", "tp", x_tp[t])
                        pe_q = pe_k = pe_v = 0
                        for which, ps in ((0, psQ), (1, psK), (2, psV)):
                            if which < 2:
                                if t >= 1:
                                    K.wait(p, "tensor", "act", (ev_q if which == 0 else ev_k)[t - 1])
                            else:
                                if t >= 1:
                                    K.wait(p, "tensor", "dve", vev[t - 1])
                            for piece in range(2):
                                col0 = which * 1024 + piece * 512
                                for c in range(NCH):
                                    if t == 0:
                                        if which < 2:
                                            K.wait(p, "tensor", "sp", wld_qk[c])
                                        else:
                                            K.wait(p, "tensor", "fld", wld_v[c])
                                    def mmf(e, c_=c, t_=t, ps_=ps, col0_=col0, piece_=piece, u_=u):
                                        return e.matmul(
                                            ps_[:, piece_ * 512:(piece_ + 1) * 512],
                                            xTr[:, u_ * 1024 + c_ * 128:u_ * 1024 + (c_ + 1) * 128],
                                            wqkv_sb[:, c_ * 3 * D + col0_: c_ * 3 * D + col0_ + 512],
                                            start=(c_ == 0), stop=False)
                                    K.run(p, "tensor", mmf)
                                if t == 0:
                                    K.wait(p, "tensor", "sp", wldA)
                                def mmA(e, ps_=ps, col0_=col0, piece_=piece):
                                    return e.matmul(ps_[:, piece_ * 512:(piece_ + 1) * 512],
                                                    ones32[:, 0:128],
                                                    wqkvA_sb[:, col0_:col0_ + 512],
                                                    start=False, stop=True)
                                v_ = K.run(p, "tensor", mmA, inc="pe")
                            if which == 0:
                                pe_q = v_
                            elif which == 1:
                                pe_k = v_
                            else:
                                pe_v = v_
                        tile_pe[t] = pe_v

                        if t >= 3:
                            K.wait(p, "scalar", "dve", mul_done[t - 3])
                            K.wait(p, "scalar", "pool", mul_done_gp[t - 3])
                        K.wait(p, "scalar", "pe", pe_q)
                        ev_q[t] = K.run(p, "scalar",
                                        lambda e, r_=r: e.copy(qraw_r[r_][:, :], psQ[:, :]), inc="act")
                        K.wait(p, "scalar", "pe", pe_k)
                        ev_k[t] = K.run(p, "scalar",
                                        lambda e, r_=r: e.copy(kraw_r[r_][:, :], psK[:, :]), inc="act")
                        K.wait(p, "vector", "pe", pe_v)
                        vev[t] = K.run(p, "vector", lambda e, t_=t: e.tensor_copy(
                            strided_ap(vt[t_], 0, 65, H, 64), psV[:, :]), inc="dve")

                        cosrep = rep_ap(cos_r[t % 4], 16, 32)
                        sinrep = rep_ap(sin_r[t % 4], 16, 32)
                        K.wait(p, "vector", "act", ev_q[t])
                        K.wait(p, "vector", "cs", cs_v)
                        if t >= 3:
                            K.wait(p, "vector", "tq", qt_tp[t - 3])
                        qr = qraw_r[r]
                        K.run(p, "vector", lambda e, raw_=qr, c_=cosrep: e.tensor_tensor(
                            rt[0][:, :], raw_[:, 0:512], c_, ALU.mult))
                        K.run(p, "vector", lambda e, raw_=qr, s_=sinrep: e.tensor_tensor(
                            rt[1][:, :], raw_[:, 512:1024], s_, ALU.mult))
                        K.run(p, "vector", lambda e, raw_=qr, s_=sinrep: e.tensor_tensor(
                            rt[2][:, :], raw_[:, 0:512], s_, ALU.mult))
                        d1 = K.run(p, "vector", lambda e, raw_=qr, c_=cosrep: e.tensor_tensor(
                            rt[3][:, :], raw_[:, 512:1024], c_, ALU.mult), inc="dve")
                        K.wait(p, "vector", "dve", d1)
                        re_q = strided_ap(rq_r[r], 0, 64, 16, 32)
                        ro_q = strided_ap(rq_r[r], 32, 64, 16, 32)
                        K.run(p, "vector", lambda e, a=re_q: e.tensor_sub(a, rt[0][:, :], rt[1][:, :]))
                        mul_done[t] = K.run(p, "vector", lambda e, a=ro_q: e.tensor_add(
                            a, rt[2][:, :], rt[3][:, :]), inc="dve")

                        def emit_qT(tq_):
                            K.wait(p, "scalar", "dve", mul_done[tq_])
                            qT_view = bass.AP(qT[:, :].tensor, qT[:, :].offset + tq_ * 128,
                                              [qT[:, :].ap[0], [S, NCH], [1, 128]])
                            K.dmat(p, "scalar", qT_view, rq_r[tq_ % 3][:, :], "tq")
                            qt_tp[tq_] = K.cnt["tq"]

                        if t >= 1:
                            emit_qT(t - 1)
                        K.wait(p, "gpsimd", "act", ev_k[t])
                        K.wait(p, "gpsimd", "cs", cs_v)
                        kr = kraw_r[r]
                        K.run(p, "gpsimd", lambda e, raw_=kr, c_=cosrep: e.tensor_tensor(
                            rtk[0][:, :], raw_[:, 0:512], c_, ALU.mult))
                        K.run(p, "gpsimd", lambda e, raw_=kr, s_=sinrep: e.tensor_tensor(
                            rtk[1][:, :], raw_[:, 512:1024], s_, ALU.mult))
                        g1 = K.run(p, "gpsimd", lambda e, raw_=kr, s_=sinrep: e.tensor_tensor(
                            rtk[2][:, :], raw_[:, 0:512], s_, ALU.mult), inc="pool")
                        mul_done_gp[t] = g1
                        K.wait(p, "vector", "act", ev_k[t])
                        d2 = K.run(p, "vector", lambda e, raw_=kr, c_=cosrep: e.tensor_tensor(
                            rtk[3][:, :], raw_[:, 512:1024], c_, ALU.mult), inc="dve")
                        if t >= 3:
                            K.wait(p, "vector", "tp", qk_tp[t - 3])
                        K.wait(p, "vector", "pool", g1)
                        K.wait(p, "vector", "dve", d2)
                        re_k = strided_ap(rk_r[r], 0, 64, 16, 32)
                        ro_k = strided_ap(rk_r[r], 32, 64, 16, 32)
                        K.run(p, "vector", lambda e, a=re_k: e.tensor_sub(a, rtk[0][:, :], rtk[1][:, :]))
                        km = K.run(p, "vector", lambda e, a=ro_k: e.tensor_add(
                            a, rtk[2][:, :], rtk[3][:, :]), inc="dve")
                        mulk_done[t] = km
                        mul_done[t] = max(mul_done[t], km)
                    emit_kT(TT - 1)
                    emit_qT(TT - 1)
                    qk_tp_done = K.cnt["tp"]
                    qt_tp_done = K.cnt["tq"]
                    vev_done = vev[TT - 1]

                with contextlib.ExitStack() as p2ctx:
                    p2sb = lambda name, shape, dt: p2ctx.enter_context(nc.sbuf_tensor(f"{name}_{b}", shape, dt))
                    stagA_r = [None, None]
                    stagB_r = [None, None]
                    Rrec_r = [None, None]
                    oA_r = [None, None]
                    oB_r = [None, None]
                    for i in (0, 1):
                        stagA_r[i] = p2sb(f"stagA{i}", [65, S], F32)
                        stagB_r[i] = p2sb(f"stagB{i}", [65, S], F32)
                        Rrec_r[i] = p2sb(f"Rrec{i}", [64, 2 * S], F32)
                        oA_r[i] = p2sb(f"oA_r{i}", [64, S], BF16)
                        oB_r[i] = p2sb(f"oB_r{i}", [64, S], BF16)
                    pk_r2 = [p2sb(f"pk_r{i}", [128, 32], F32) for i in range(2)]
                    pk_o2 = [p2sb(f"pk_o{i}", [128, 32], F32) for i in range(2)]
                    ex_r = [p2sb(f"ex{i}", [128, 1024], BF16) for i in range(3)]
                    psS_r = [p2ctx.enter_context(nc.psum_tensor(f"psSr{i}_{b}", [128, 1024], F32)) for i in range(3)]
                    psOA = p2ctx.enter_context(nc.psum_tensor(f"psOA_{b}", [65, 512], F32))
                    psOB = p2ctx.enter_context(nc.psum_tensor(f"psOB_{b}", [65, 512], F32))

                    ostore = {}
                    norm_pool_hist = {}
                    recip_hist = {}
                    sums_hist = {}
                    pkl_hist = {}
                    ev_done = 0

                    def emit_recip_load(gg):
                        pk_r = pk_r2[gg % 2]
                        K.wait(p, "gpsimd", "sums", sums_hist[gg])
                        if gg >= 2:
                            K.wait(p, "gpsimd", "pks", K.cnt["pks"] - 16)
                            K.wait(p, "gpsimd", "act", recip_hist.get(gg - 2, 0))
                            K.wait(p, "gpsimd", "pool", norm_pool_hist.get(gg - 2, 0))
                        base = sums_d.offset + (b * NHG + gg) * 2 * S
                        pkl_hist[gg] = K.dma(p, "gpsimd", pk_r[:, :],
                                             bass.AP(sums_d.tensor, base,
                                                     [[16, 128], [S, 2], [1, 16]]), "pkl")

                    def emit_recip_compute(gg):
                        nonlocal rload_last_gp
                        stagA, stagB = stagA_r[gg % 2], stagB_r[gg % 2]
                        Rrec = Rrec_r[gg % 2]
                        pk_r, pk_o = pk_r2[gg % 2], pk_o2[gg % 2]
                        K.wait(p, "scalar", "pkl", pkl_hist[gg])
                        K.run(p, "scalar", lambda e, a=pk_r: e.activation(
                            a[:, :], a[:, :], AF.Ln, bias=epssm[:, :]))
                        rc = K.run(p, "scalar", lambda e, a=pk_r, o=pk_o: e.activation(
                            o[:, :], a[:, :], AF.Exp, scale=-1.0), inc="act")
                        recip_hist[gg] = rc
                        K.wait(p, "gpsimd", "act", rc)
                        rbase = rrec_d.offset + (b * NHG + gg) * 2 * S
                        pks_v = K.dma(p, "gpsimd",
                                      bass.AP(rrec_d.tensor, rbase,
                                              [[16, 128], [S, 2], [1, 16]]),
                                      pk_o[:, :], "pks")
                        K.wait(p, "gpsimd", "pks", pks_v)
                        rload = K.dma(p, "gpsimd", Rrec[:, :],
                                      bass.AP(rrec_d.tensor, rbase,
                                              [[0, 64], [S, 2], [1, S]]), "rld")
                        rload_last_gp = rload
                        orrA, orrB = oA_r[gg % 2], oB_r[gg % 2]
                        K.wait(p, "gpsimd", "rld", rload)
                        if gg >= 2:
                            K.wait(p, "gpsimd", *ostore[gg - 2])
                        for j4 in range(4):
                            c4 = j4 * 512
                            K.run(p, "gpsimd", lambda e, o_=orrA, sA=stagA, R=Rrec, c_=c4:
                                  e.tensor_mul(o_[:, c_:c_ + 512], sA[0:64, c_:c_ + 512],
                                               R[:, c_:c_ + 512]))
                            onorm = K.run(p, "gpsimd", lambda e, o_=orrB, sB=stagB, R=Rrec, c_=c4: e.tensor_mul(
                                o_[:, c_:c_ + 512], sB[0:64, c_:c_ + 512], R[:, S + c_:S + c_ + 512]),
                                inc="pool" if j4 == 3 else None)
                        norm_pool_hist[gg] = onorm
                        ofam = "os2" if gg < NHG - 1 else "ost"
                        if gg == 0:
                            K.wait(p, "gpsimd", "sp", osl_last_sp)
                        K.dma(p, "gpsimd", oT_d[b, gg][0:64, :], orrA[:, :], ofam)
                        ostore[gg] = (ofam, K.dma(p, "gpsimd", oT_d[b, gg][64:128, :], orrB[:, :], ofam))


                    items = []
                    for g_ in range(NHG):
                        for qb_ in range(4):
                            nkt_ = (qb_ * 512 + 512) // 128
                            for kt_ in range(nkt_):
                                items.append((g_, qb_, kt_, kt_ == nkt_ - 1))
                    NIT = len(items)
                    exp_done_f = {}
                    av_done_f = {}
                    av_ready_f = {}
                    sums_st = 0

                    def emit_av_f(j):
                        nonlocal ev_done, sums_st
                        g_, qb_, kt_, last_ = items[j]
                        q0_ = qb_ * 512
                        coff = max(0, kt_ * 128 - q0_)
                        need_dve, need_act = av_ready_f[j]
                        if need_dve:
                            K.wait(p, "tensor", "dve", need_dve)
                        else:
                            K.wait(p, "tensor", "act", need_act)
                        if kt_ == 0:
                            K.wait(p, "tensor", "dve", ev_done)
                        ex = ex_r[j % 3]
                        w_ = 512 - coff
                        for i_, (head, psO) in enumerate(((0, psOA), (1, psOB))):
                            def avf(e, head_=head, psO_=psO, ex_=ex, c0_=coff, w__=w_,
                                    kt2_=kt_, g2_=g_):
                                return e.matmul(
                                    psO_[:, c0_:c0_ + w__],
                                    vt[kt2_][:, (2 * g2_ + head_) * 65:(2 * g2_ + head_) * 65 + 65],
                                    ex_[:, head_ * 512 + c0_:head_ * 512 + c0_ + w__],
                                    start=(kt2_ == 0), stop=False)
                            K.run(p, "tensor", avf, inc="pe" if i_ == 1 else None)
                        av_done_f[j] = K.cnt["pe"]
                        if last_:
                            stagA = stagA_r[g_ % 2]
                            stagB = stagB_r[g_ % 2]
                            K.wait(p, "vector", "pe", av_done_f[j])
                            if qb_ == 0 and g_ >= 2:
                                K.wait(p, "vector", "pool", norm_pool_hist.get(g_ - 2, 0))
                            K.run(p, "vector", lambda e, q0__=q0_, sA=stagA: e.tensor_copy(
                                sA[:, q0__:q0__ + 512], psOA[:, :]))
                            ev_done = K.run(p, "vector", lambda e, q0__=q0_, sB=stagB: e.tensor_copy(
                                sB[:, q0__:q0__ + 512], psOB[:, :]), inc="dve")
                            K.wait(p, "gpsimd", "dve", ev_done)
                            if g_ == 0 and qb_ == 0:
                                K.wait(p, "gpsimd", "rld", rload_last_gp)
                            K.dma(p, "gpsimd", sums_d[b, g_, 0, q0_:q0_ + 512],
                                  stagA[64:65, q0_:q0_ + 512], "sums")
                            sums_st = K.dma(p, "gpsimd", sums_d[b, g_, 1, q0_:q0_ + 512],
                                            stagB[64:65, q0_:q0_ + 512], "sums")
                            if qb_ == 3:
                                sums_hist[g_] = sums_st

                    for j, (g, qb, kt, last) in enumerate(items):
                        q0 = qb * 512
                        coff = max(0, kt * 128 - q0)
                        if kt == 0:
                            if qb == 1 and g >= 1:
                                emit_recip_load(g - 1)
                            if qb == 2 and g >= 1:
                                emit_recip_compute(g - 1)
                            if qb == 0 and g == 0:
                                K.wait(p, "tensor", "dve", K.cnt["dve"])
                                K.wait(p, "tensor", "act", K.cnt["act"])
                        if g == 0:
                            K.wait(p, "tensor", "tp", qk_tp[kt])
                            K.wait(p, "tensor", "tq", qt_tp[min(4 * qb + 3, TT - 1)])
                        if j >= 3:
                            K.wait(p, "tensor", "act", exp_done_f[j - 3])
                        w_ = 512 - coff
                        for head, base, tpos in ((0, 0, (0, 0)), (1, 64, (64, 0))):
                            def mmf(e, head_=head, base_=base, tpos_=tpos,
                                    c0_=coff, w__=w_, kt_=kt, q0_=q0, g_=g, j_=j):
                                return e.matmul(
                                    psS_r[j_ % 3][:, head_ * 512 + c0_:head_ * 512 + c0_ + w__],
                                    kT[base_:base_ + 64, g_ * S + kt_ * 128:g_ * S + (kt_ + 1) * 128],
                                    qT[base_:base_ + 64, g_ * S + q0_ + c0_:g_ * S + q0_ + c0_ + w__],
                                    start=True, stop=True, tile_position=tpos_)
                            K.run(p, "tensor", mmf, inc="pe")
                        pe_sc = K.cnt["pe"]
                        if j >= 2:
                            emit_av_f(j - 2)
                        K.wait(p, "scalar", "pe", pe_sc)
                        if j >= 3:
                            K.wait(p, "scalar", "pe", av_done_f.get(j - 3, 0))
                        exd = ex_r[j % 3]
                        def exf(e, ex_=exd, coff_=coff, ps_=psS_r[j % 3]):
                            pap = ps_[:, :].ap
                            xap = ex_[:, :].ap
                            src = bass.AP(ps_[:, :].tensor, ps_[:, :].offset + coff_,
                                          [pap[0], [512, 2], [1, 512 - coff_]])
                            dst = bass.AP(ex_[:, :].tensor, ex_[:, :].offset + coff_,
                                          [xap[0], [512, 2], [1, 512 - coff_]])
                            return e.activation(dst, src, AF.Exp, scale=0.125)
                        K.run(p, "scalar", exf, inc="act")
                        exp_done_f[j] = K.cnt["act"]
                        if kt * 128 >= q0:
                            K.wait(p, "vector", "act", exp_done_f[j])
                            def mkf(e, ex_=exd, c_=coff):
                                xap = ex_[:, :].ap
                                v2 = bass.AP(ex_[:, :].tensor, ex_[:, :].offset + c_,
                                             [xap[0], [512, 2], [1, 128]])
                                cm2 = bass.AP(cmask[:, :].tensor, cmask[:, :].offset,
                                              [cmask[:, :].ap[0], [0, 2], [1, 128]])
                                return e.tensor_tensor(v2, v2, cm2, ALU.mult)
                            mask_cnt = K.run(p, "vector", mkf, inc="dve")
                            av_ready_f[j] = (mask_cnt, None)
                        else:
                            av_ready_f[j] = (None, exp_done_f[j])
                    emit_av_f(NIT - 2)
                    emit_av_f(NIT - 1)
                    emit_recip_load(NHG - 1)
                    emit_recip_compute(NHG - 1)
                    oT_stores_done = K.cnt["ost"]

                with contextlib.ExitStack() as p3ctx:
                    p3 = lambda name, shape, dt: p3ctx.enter_context(nc.sbuf_tensor(f"{name}_{b}", shape, dt))
                    wo_sb = p3("wo_sb", [128, NCH * D], BF16)
                    woA_sb = p3("woA_sb", [32, D], BF16)
                    osl_r = [p3(f"osl{i}", [128, NCH * 128], BF16) for i in range(3)]
                    xf3_r = [p3(f"xf3{i}", [128, D], F32) for i in range(3)]
                    y_r = [p3(f"y{i}", [128, D], F32) for i in range(2)]
                    ln1f = [p3(f"ln1f{i}", [128, D], F32) for i in range(4)]
                    ln1b_r = [p3(f"ln1b{i}", [128, D], BF16) for i in range(2)]
                    ln1T = p3("ln1T", [128, NCH * 512], BF16)
                    h_r = [p3(f"h{i}", [128, 512], BF16) for i in range(2)]
                    gA = p3("gA", [96, 512], BF16)
                    gB = p3("gB", [128, 512], F32)
                    gC = p3("gC", [128, 512], F32)
                    godd = p3("godd", [64, 512], F32)
                    yo_r = [p3(f"yo{i}", [128, D], F32) for i in range(2)]
                    y2_r = [p3(f"y2_{i}", [128, D], F32) for i in range(2)]
                    sq_scr = p3("sq_scr", [128, D], BF16)
                    psP = p3ctx.enter_context(nc.psum_tensor(f"psP_{b}", [128, 1024], F32))
                    psS2 = p3ctx.enter_context(nc.psum_tensor(f"psS2_{b}", [128, 512], F32))
                    psF_r = [p3ctx.enter_context(nc.psum_tensor(f"psF{i}_{b}", [128, 1024], F32))
                             for i in range(2)]

                    K.wait(p, "sync", "pool", norm_pool_hist[NHG - 2])
                    K.wait(p, "scalar", "pool", norm_pool_hist[NHG - 2])
                    K.wait(p, "vector", "pool", K.cnt["pool"])
                    woldA = K.dma(p, "sync", woA_sb[:, :], wo_bf[D:D + 32, :], "sp")
                    wold_c = [0] * NCH
                    for c in range(NCH):
                        if c % 2 == 0:
                            wold_c[c] = ("sp", K.dma(p, "sync", wo_sb[:, c * D:(c + 1) * D],
                                                     wo_bf[c * 128:(c + 1) * 128, :], "sp"))
                        else:
                            wold_c[c] = ("fld", K.dma(p, "scalar", wo_sb[:, c * D:(c + 1) * D],
                                                      wo_bf[c * 128:(c + 1) * 128, :], "fld"))
                    K.run(p, "vector", lambda e: e.memset(gA[64:96, :], 0.0))
                    K.run(p, "vector", lambda e: e.memset(gA[64:65, :], 1.0), inc="dve")

                    resid_done = {}
                    resid2_done = {}
                    opmm_done = {}
                    store_done = {}
                    ln1_tp = {}
                    sq1 = {}
                    sq2 = {}
                    norm1_pool = {}
                    norm2_pool = {}
                    oldvs = {}
                    xldvs = {}
                    ffnout_done = 0
                    ffn_sproj_done = 0
                    gchain_done = 0

                    def prefetch_slice(s_):
                        rr = s_ % 3
                        if s_ >= 3:
                            K.wait(p, "sync", "dve", resid_done.get(s_ - 3, 0))
                            K.wait(p, "sync", "pe", opmm_done.get(s_ - 3, 0))
                        ol = 0
                        for c in range(NCH):
                            if s_ <= 3:
                                fam, cnt = ostore.get(c, ("ost", 0))
                                K.wait(p, "sync", fam, cnt)
                            ol = K.dma(p, "sync", osl_r[rr][:, c * 128:(c + 1) * 128],
                                       oT_d[b, c][:, s_ * 128:(s_ + 1) * 128], "sp")
                        oldvs[s_] = ol
                        if s_ <= 3:
                            K.wait(p, "sync", *ostore.get(NHG - 2, ("ost", 0)))
                        xldvs[s_] = K.dma(p, "sync", xf3_r[rr][:, :],
                                          x_in[row0 + s_ * 128: row0 + (s_ + 1) * 128, :], "sp")

                    def finish_ln1(s_):
                        si_ = s_ % 4
                        r2_ = s_ % 2
                        norm1_pool[s_] = _ln_partB(
                            K, p, y_r[r2_], statcols(r2_), sq1[s_],
                            outs=[ln1f[si_][:, :], ln1b_r[r2_][:, :]], epscol=epscol,
                            pool_waits=[("dve", resid2_done.get(s_ - 4, 0)),
                                        ("tq", ln1_tp.get(s_ - 2, 0))])
                        K.wait(p, "scalar", "pool", norm1_pool[s_])
                        K.wait(p, "scalar", "pe", ffn_sproj_done)
                        lnT_view = bass.AP(ln1T[:, :].tensor, ln1T[:, :].offset + si_ * 128,
                                           [ln1T[:, :].ap[0], [512, NCH], [1, 128]])
                        K.dmat(p, "scalar", lnT_view, ln1b_r[r2_][:, :], "tq")
                        ln1_tp[s_] = K.cnt["tq"]

                    def finish_ln2(s_):
                        si_ = s_ % 4
                        r2_ = s_ % 2
                        norm2_pool[s_] = _ln_partB(
                            K, p, y2_r[r2_], statcols2(r2_), sq2[s_],
                            outs=[yo_r[r2_][:, :]], epscol=epscol,
                            pool_waits=[("st", store_done.get(s_ - 2, 0))])
                        K.wait(p, "gpsimd", "pool", norm2_pool[s_])
                        store_done[s_] = K.dma(p, "gpsimd",
                                               out_d[row0 + s_ * 128: row0 + (s_ + 1) * 128, :],
                                               yo_r[r2_][:, :], "st")

                    def emit_ffn_head(jj):
                        nonlocal ffn_sproj_done, gchain_done
                        K.wait(p, "tensor", "tq", ln1_tp[4 * jj + 3])
                        K.wait(p, "tensor", "dve", gchain_done)
                        for c in range(NCH):
                            def mmf(e, c_=c):
                                return e.matmul(psS2[:, :], win_sb[:, c_ * 128:(c_ + 1) * 128],
                                                ln1T[:, c_ * 512:(c_ + 1) * 512],
                                                start=(c_ == 0), stop=False)
                            K.run(p, "tensor", mmf)
                        ffn_sproj_done = K.run(p, "tensor", lambda e: e.matmul(
                            psS2[:, :], winA_sb[:, :], ones32[0:32, 0:512],
                            start=False, stop=True), inc="pe")
                        gstep = {}
                        hv = {}
                        for i in range(8):
                            K.wait(p, "scalar", "pe", ffn_sproj_done)
                            if i >= 2:
                                K.wait(p, "scalar", "dve", gstep[i - 2])
                            def hf(e, i_=i, hr=h_r[i % 2]):
                                return e.activation(hr[:, :], psS2[:, :], AF.Relu,
                                                    bias=b1_sb[:, i_:i_ + 1], scale=w1_sb[:, i_:i_ + 1])
                            hv[i] = K.run(p, "scalar", hf, inc="act")
                            K.wait(p, "vector", "act", hv[i])
                            if i == 0:
                                K.wait(p, "vector", "pe", ffnout_done)
                                K.wait(p, "vector", "fld", K.cnt["fld"])
                                def g0(e):
                                    return e.tensor_scalar(gB[:, :], h_r[0][:, :],
                                                           w2_sb[:, 0:1], b2_sb[:, :], ALU.mult, ALU.add)
                                gstep[0] = K.run(p, "vector", g0, inc="dve")
                            else:
                                K.wait(p, "vector", "dve", gstep[i - 1])
                                src, dst = (gB[:, :], gC[:, :]) if i % 2 == 1 else (gC[:, :], gB[:, :])
                                def gi(e, i_=i, s_=src, d_=dst, hr=h_r[i % 2]):
                                    return e.scalar_tensor_tensor(d_, hr[:, :], w2_sb[:, i_:i_ + 1],
                                                                  s_, ALU.mult, ALU.add)
                                gstep[i] = K.run(p, "vector", gi, inc="dve")
                        K.wait(p, "scalar", "dve", gstep[7])
                        fold_v = K.dma(p, "scalar", godd[:, :], gC[64:128, :], "fld")
                        K.wait(p, "vector", "fld", fold_v)
                        gchain_done = K.run(p, "vector", lambda e: e.tensor_add(
                            gA[0:64, :], gC[0:64, :], godd[:, :]), inc="dve")

                    def emit_ffn_tail(jj):
                        nonlocal ffnout_done
                        for si in range(4):
                            s = 4 * jj + si
                            r2 = s % 2
                            K.wait(p, "tensor", "dve", gchain_done)
                            if s >= 2:
                                K.wait(p, "tensor", "dve", resid2_done.get(s - 2, 0))
                            pe_f = 0
                            for piece in range(2):
                                def mmf(e, si_=si, piece_=piece, ps_=psF_r[s % 2]):
                                    return e.matmul(ps_[:, piece_ * 512:(piece_ + 1) * 512],
                                                    gA[:, si_ * 128:(si_ + 1) * 128],
                                                    wf_sb[:, piece_ * 512:piece_ * 512 + 512],
                                                    start=True, stop=True)
                                pe_f = K.run(p, "tensor", mmf, inc="pe")
                            if si == 3:
                                ffnout_done = pe_f
                            K.wait(p, "vector", "pool", norm1_pool[s])
                            K.wait(p, "vector", "pool", norm2_pool.get(s - 2, 0))
                            K.wait(p, "scalar", "pool", norm2_pool.get(s - 2, 0))
                            resid2_done[s], sq2[s] = _ln_partA(
                                K, p, psF_r[s % 2], ln1f[si], y2_r[r2], statcols2(r2), sq_scr, pe_f)
                            if si >= 1:
                                finish_ln2(s - 1)
                        finish_ln2(4 * jj + 3)

                    for s0 in range(3):
                        prefetch_slice(s0)
                    for j in range(4):
                        for si in range(4):
                            s = 4 * j + si
                            r = s % 3
                            r2 = s % 2
                            if s == 0:
                                K.wait(p, "tensor", "dve", K.cnt["dve"])
                            K.wait(p, "tensor", "sp", oldvs[s])
                            if s >= 1:
                                K.wait(p, "tensor", "dve", resid_done.get(s - 1, 0))
                            for piece in range(2):
                                for c in range(NCH):
                                    if s == 0:
                                        K.wait(p, "tensor", wold_c[c][0], wold_c[c][1])
                                    def mmf(e, c_=c, r_=r, piece_=piece):
                                        return e.matmul(psP[:, piece_ * 512:(piece_ + 1) * 512],
                                                        osl_r[r_][:, c_ * 128:(c_ + 1) * 128],
                                                        wo_sb[:, c_ * D + piece_ * 512: c_ * D + piece_ * 512 + 512],
                                                        start=(c_ == 0), stop=False)
                                    K.run(p, "tensor", mmf)
                                if s == 0:
                                    K.wait(p, "tensor", "sp", woldA)
                                def mmA(e, piece_=piece):
                                    return e.matmul(psP[:, piece_ * 512:(piece_ + 1) * 512],
                                                    ones32[:, 0:128],
                                                    woA_sb[:, piece_ * 512:piece_ * 512 + 512],
                                                    start=False, stop=True)
                                K.run(p, "tensor", mmA, inc="pe")
                            opmm_done[s] = pe_op = K.cnt["pe"]
                            K.wait(p, "vector", "sp", xldvs[s])
                            K.wait(p, "vector", "pool", norm1_pool.get(s - 2, 0))
                            K.wait(p, "scalar", "pool", norm1_pool.get(s - 2, 0))
                            resid_done[s], sq1[s] = _ln_partA(
                                K, p, psP, xf3_r[r], y_r[r2], statcols(r2), sq_scr, pe_op)
                            if s + 3 < TT:
                                prefetch_slice(s + 3)
                            if si == 1 and j >= 1:
                                emit_ffn_head(j - 1)
                                emit_ffn_tail(j - 1)
                            if si >= 1:
                                finish_ln1(s - 1)
                        finish_ln1(4 * j + 3)
                    emit_ffn_head(3)
                    emit_ffn_tail(3)

        K.wait(p, "sync", "st", K.cnt["st"])
        K.flush(p)

    return nc


_PROG = None


def _get_prog():
    global _PROG
    if _PROG is None:
        _PROG = build_program()
    return _PROG


def _pair8(w):
    return np.ascontiguousarray(np.concatenate([w[:, 0::2], w[:, 1::2]], axis=0))


def _host_arrays(L0_attn_w_qkv, L0_attn_b_qkv, L0_attn_w_out, L0_attn_b_out,
                 L0_ffn_Win, L0_ffn_bin_, L0_ffn_W1, L0_ffn_b1, L0_ffn_W2,
                 L0_ffn_b2, L0_ffn_Wout, L0_ffn_bout):
    f32 = np.float32
    wqkv = np.asarray(L0_attn_w_qkv, f32)
    bqkv = np.asarray(L0_attn_b_qkv, f32)
    hh = np.arange(H)
    pp = np.arange(32)
    e_idx = (hh[:, None] * 64 + 2 * pp[None, :]).ravel()
    o_idx = e_idx + 1
    q_perm = np.concatenate([e_idx, o_idx])
    qk_cols = np.concatenate([q_perm, q_perm + D])
    wqk = np.zeros((D + 32, 2 * D), f32)
    wqk[0:D, :] = wqkv[qk_cols, :].T
    wqk[D, :] = bqkv[qk_cols]
    wv = np.zeros((D + 32, D), f32)
    wv[0:D, :] = wqkv[2 * D:3 * D, :].T
    wv[D, :] = bqkv[2 * D:3 * D]
    wo = np.zeros((D + 32, D), f32)
    wo[0:D, :] = np.asarray(L0_attn_w_out, f32).T
    wo[D, :] = np.asarray(L0_attn_b_out, f32)
    win = np.zeros((D + 32, 128), f32)
    win[0:D, 0:64] = np.asarray(L0_ffn_Win, f32)
    win[D, 0:64] = np.asarray(L0_ffn_bin_, f32)
    win[:, 64:128] = win[:, 0:64]
    wf = np.zeros((96, D), f32)
    wf[0:64, :] = np.asarray(L0_ffn_Wout, f32)
    wf[64, :] = np.asarray(L0_ffn_bout, f32)
    ang = (np.arange(S, dtype=np.float64)[:, None]
           * (10000.0 ** (-np.arange(0, DK, 2, dtype=np.float64) / DK))[None, :])
    cosT = np.cos(ang).astype(np.float16)
    sinT = np.sin(ang).astype(np.float16)
    cmask = np.triu(np.ones((128, 128), np.float32)).astype(ml_dtypes.bfloat16)
    return {
        "wqk": np.ascontiguousarray(wqk),
        "wv": np.ascontiguousarray(wv),
        "wo": np.ascontiguousarray(wo),
        "win": np.ascontiguousarray(win),
        "wf": np.ascontiguousarray(wf),
        "w1": _pair8(np.asarray(L0_ffn_W1, f32)),
        "b1": _pair8(np.asarray(L0_ffn_b1, f32)),
        "w2": _pair8(np.asarray(L0_ffn_W2, f32)),
        "binv": np.ascontiguousarray(np.asarray(L0_ffn_bin_, f32)[:, None]),
        "b2": np.ascontiguousarray(np.concatenate(
            [np.asarray(L0_ffn_b2, f32), np.zeros(64, f32)])[:, None]),
        "cosT": cosT, "sinT": sinT, "cmask": cmask,
    }


def kernel(L0_attn_x, L0_attn_w_qkv, L0_attn_b_qkv, L0_attn_w_out, L0_attn_b_out,
           L0_ffn_Win, L0_ffn_bin_, L0_ffn_W1, L0_ffn_b1, L0_ffn_W2, L0_ffn_b2,
           L0_ffn_Wout, L0_ffn_bout):
    x = np.ascontiguousarray(np.asarray(L0_attn_x, np.float32))
    shared = _host_arrays(L0_attn_w_qkv, L0_attn_b_qkv, L0_attn_w_out, L0_attn_b_out,
                          L0_ffn_Win, L0_ffn_bin_, L0_ffn_W1, L0_ffn_b1, L0_ffn_W2,
                          L0_ffn_b2, L0_ffn_Wout, L0_ffn_bout)
    nc = _get_prog()
    in_maps = []
    for core in range(NCORES):
        m = dict(shared)
        m["x"] = np.ascontiguousarray(x[core * BPC:(core + 1) * BPC].reshape(BPC * S, D))
        in_maps.append(m)
    res = run_bass_kernel_spmd(nc, in_maps, core_ids=list(range(NCORES)))
    out = np.concatenate([r["out"].reshape(BPC, S, D) for r in res.results], axis=0)
    return out.astype(np.float32)


# version 50
# speedup vs baseline: 1.1913x; 1.1913x over previous
import contextlib
import numpy as np
import ml_dtypes

import concourse.bass as bass
import concourse.mybir as mybir
from concourse.bass_utils import run_bass_kernel_spmd

F32 = mybir.dt.float32
BF16 = mybir.dt.bfloat16
F16 = mybir.dt.float16
AF = mybir.ActivationFunctionType
ALU = mybir.AluOpType

B, S, D, H, DK = 32, 2048, 1024, 16, 64
NCORES = 8
BPC = B // NCORES
TT = S // 128
NCH = D // 128
NHG = H // 2
EPS_LN = 1e-5
EPS_SM = 1e-8
ACT_SET_LN_EXP = 6

ENGINES = ("tensor", "scalar", "vector", "gpsimd", "sync")


class Plan:
    def __init__(self):
        self.items = {e: [] for e in ENGINES}

    def add(self, eng, fn):
        self.items[eng].append(fn)


class KB:
    def __init__(self, nc, ctx):
        self.nc = nc
        self.sems = {}
        self.cnt = {}
        for name in ("sp", "gp", "pe", "act", "dve", "pool", "st", "tp", "ost", "os2", "sums",
                     "rld", "pkl", "pks", "fld", "cs", "tq"):
            self.sems[name] = ctx.enter_context(nc.semaphore("s_" + name))
            self.cnt[name] = 0

    def wait(self, p, eng, sem, val=None):
        v = self.cnt[sem] if val is None else val
        if v > 0:
            p.add(eng, lambda e, s=self.sems[sem], vv=v: e.wait_ge(s, vv))
        return v

    def run(self, p, eng, fn, inc=None, by=1):
        if inc is None:
            p.add(eng, fn)
            return None
        p.add(eng, lambda e, s=self.sems[inc], f=fn, b=by: f(e).then_inc(s, b))
        self.cnt[inc] += by
        return self.cnt[inc]

    def dma(self, p, eng, out, in_, sem):
        return self.run(p, eng, lambda e, o=out, i=in_: e.dma_start(o, i), inc=sem, by=16)

    def dmat(self, p, eng, out, in_, sem):
        return self.run(p, eng, lambda e, o=out, i=in_: e.dma_start_transpose(o, i), inc=sem, by=16)

    def soft_barrier(self, p, engines=ENGINES):
        for eng in engines:
            for sem in self.cnt:
                self.wait(p, eng, sem)

    def flush(self, plan):
        with self.nc.Block() as blk:
            for eng in ENGINES:
                fns = plan.items[eng]
                if fns:
                    getattr(blk, eng)(lambda e, fl=tuple(fns): [f(e) for f in fl])


def rep_ap(t, n_rep, width):
    ap = t[:, :].ap
    return bass.AP(t[:, :].tensor, t[:, :].offset, [ap[0], [0, n_rep], [1, width]])


def strided_ap(t, offset_cols, blk_stride, n_blk, width):
    ap = t[:, :].ap
    return bass.AP(t[:, :].tensor, t[:, :].offset + offset_cols,
                   [ap[0], [blk_stride, n_blk], [1, width]])


def _ln_partA(K, p, ps, resid_in, y, cols, sq_scr, pe_wait):
    mc, ssqc = cols[0], cols[1]
    K.wait(p, "vector", "pe", pe_wait)
    resid_cnt = K.run(p, "vector", lambda e: e.scalar_tensor_tensor(
        y[:, :], ps[:, :], 1.0, resid_in[:, :], ALU.mult, ALU.add), inc="dve")
    K.wait(p, "scalar", "dve", resid_cnt)
    K.run(p, "scalar", lambda e: e.activation(
        sq_scr[:, :], y[:, :], AF.Copy, scale=1.0 / D, accum_out=mc[:, :]))
    sq_cnt = K.run(p, "scalar", lambda e: e.activation(
        sq_scr[:, :], y[:, :], AF.Square, scale=D ** -0.5, accum_out=ssqc[:, :]), inc="act")
    return resid_cnt, sq_cnt


def _ln_partB(K, p, y, cols, sq_cnt, outs, epscol, pool_waits=()):
    mc, ssqc, msqc, varc, rstdc, bcolc = cols
    K.wait(p, "vector", "act", sq_cnt)
    m_cnt = K.run(p, "vector", lambda e: e.tensor_mul(msqc[:, :], mc[:, :], mc[:, :]), inc="dve")
    K.wait(p, "vector", "dve", m_cnt)
    var_cnt = K.run(p, "vector", lambda e: e.tensor_scalar(
        varc[:, :], msqc[:, :], -1.0, ssqc[:, :], ALU.mult, ALU.add), inc="dve")
    K.wait(p, "scalar", "dve", var_cnt)
    ln_cnt = K.run(p, "scalar", lambda e: e.activation(
        varc[:, :], varc[:, :], AF.Ln, bias=epscol[:, :]), inc="act")
    K.wait(p, "scalar", "act", ln_cnt)
    rstd_cnt = K.run(p, "scalar", lambda e: e.activation(
        rstdc[:, :], varc[:, :], AF.Exp, scale=-0.5), inc="act")
    K.wait(p, "vector", "act", rstd_cnt)
    b_cnt = K.run(p, "vector", lambda e: e.scalar_tensor_tensor(
        bcolc[:, :], mc[:, :], -1.0, rstdc[:, :], ALU.mult, ALU.mult), inc="dve")
    K.wait(p, "gpsimd", "act", rstd_cnt)
    K.wait(p, "gpsimd", "dve", b_cnt)
    for sem, val in pool_waits:
        K.wait(p, "gpsimd", sem, val)
    done = None
    for tgt in outs:
        done = K.run(p, "gpsimd", lambda e, t_=tgt: e.tensor_scalar(
            t_, y[:, :], rstdc[:, :], bcolc[:, :], ALU.mult, ALU.add), inc="pool")
    return done


DEBUG_TAPS = False


SERIAL_P34 = False


def build_program():
    nc = bass.Bass("TRN2", target_bir_lowering=False)

    x_in = nc.dram_tensor("x", [BPC * S, D], F32, kind="ExternalInput").ap()
    wqk_in = nc.dram_tensor("wqk", [D + 32, 2 * D], F32, kind="ExternalInput").ap()
    wv_in = nc.dram_tensor("wv", [D + 32, D], F32, kind="ExternalInput").ap()
    wo_in = nc.dram_tensor("wo", [D + 32, D], F32, kind="ExternalInput").ap()
    win_in = nc.dram_tensor("win", [D + 32, 128], F32, kind="ExternalInput").ap()
    wf_in = nc.dram_tensor("wf", [96, D], F32, kind="ExternalInput").ap()
    w1_in = nc.dram_tensor("w1", [128, 8], F32, kind="ExternalInput").ap()
    b1_in = nc.dram_tensor("b1", [128, 8], F32, kind="ExternalInput").ap()
    w2_in = nc.dram_tensor("w2", [128, 8], F32, kind="ExternalInput").ap()
    bin_in = nc.dram_tensor("binv", [64, 1], F32, kind="ExternalInput").ap()
    b2_in = nc.dram_tensor("b2", [128, 1], F32, kind="ExternalInput").ap()
    cos_in = nc.dram_tensor("cosT", [S, 32], F16, kind="ExternalInput").ap()
    sin_in = nc.dram_tensor("sinT", [S, 32], F16, kind="ExternalInput").ap()
    cm_in = nc.dram_tensor("cmask", [128, 128], BF16, kind="ExternalInput").ap()
    out_d = nc.dram_tensor("out", [BPC * S, D], F32, kind="ExternalOutput").ap()

    scratch_kind = "ExternalOutput" if DEBUG_TAPS else "Internal"
    wqk_bf = nc.dram_tensor("wqk_bf", [D + 32, 2 * D], BF16, kind=scratch_kind).ap()
    wv_bf = nc.dram_tensor("wv_bf", [D + 32, D], BF16, kind="Internal").ap()
    wo_bf = nc.dram_tensor("wo_bf", [D + 32, D], BF16, kind="Internal").ap()
    sums_d = nc.dram_tensor("sums_d", [BPC, NHG, 2, S], F32, kind=scratch_kind).ap()
    rrec_d = nc.dram_tensor("rrec_d", [BPC, NHG, 2, S], F32, kind="Internal").ap()
    oT_d = nc.dram_tensor("oT_d", [BPC, NHG, 128, S], BF16, kind=scratch_kind).ap()
    if DEBUG_TAPS:
        dbg_qraw = nc.dram_tensor("dbg_qraw", [128, D], F32, kind="ExternalOutput").ap()
        dbg_rq = nc.dram_tensor("dbg_rq", [128, D], F32, kind="ExternalOutput").ap()
        dbg_qT = nc.dram_tensor("dbg_qT", [128, 256], F32, kind="ExternalOutput").ap()
        dbg_vt = nc.dram_tensor("dbg_vt", [128, H * 65], F32, kind="ExternalOutput").ap()
        dbg_ex = nc.dram_tensor("dbg_ex", [128, 1024], F32, kind="ExternalOutput").ap()
        dbg_stagA = nc.dram_tensor("dbg_stagA", [65, S], F32, kind="ExternalOutput").ap()
        dbg_ln1 = nc.dram_tensor("dbg_ln1", [128, D], F32, kind="ExternalOutput").ap()
        dbg_xT = nc.dram_tensor("dbg_xT", [128, 256], F32, kind="ExternalOutput").ap()
        dbg_osl = nc.dram_tensor("dbg_osl", [128, NCH * 128], F32, kind="ExternalOutput").ap()
        dbg_y = nc.dram_tensor("dbg_y", [128, D], F32, kind="ExternalOutput").ap()
        dbg_stat = nc.dram_tensor("dbg_stat", [4, 128], F32, kind="ExternalOutput").ap()
        dbg_lnT = nc.dram_tensor("dbg_lnT", [128, 512], F32, kind="ExternalOutput").ap()
        dbg_g = nc.dram_tensor("dbg_g", [96, 512], F32, kind="ExternalOutput").ap()
        dbg_s2 = nc.dram_tensor("dbg_s2", [64, 512], F32, kind="ExternalOutput").ap()
        dbg_y2 = nc.dram_tensor("dbg_y2", [128, D], F32, kind="ExternalOutput").ap()
        dbg_h0 = nc.dram_tensor("dbg_h0", [64, 512], F32, kind="ExternalOutput").ap()
        dbg_win = nc.dram_tensor("dbg_win", [128, 512], F32, kind="ExternalOutput").ap()
        dbg_ya = nc.dram_tensor("dbg_ya", [BPC, TT, 128, D], F32, kind="ExternalOutput").ap()
        dbg_osla = nc.dram_tensor("dbg_osla", [BPC, TT, 128, D], BF16, kind="ExternalOutput").ap()

    with contextlib.ExitStack() as ctx:
        K = KB(nc, ctx)
        sb = lambda name, shape, dt: ctx.enter_context(nc.sbuf_tensor(name, shape, dt))

        ones32 = sb("ones32", [32, 512], BF16)
        cmask = sb("cmask_sb", [128, 128], BF16)
        win_sb = sb("win_sb", [128, NCH * 128], BF16)
        winA_sb = sb("winA_sb", [32, 128], BF16)
        wf_sb = sb("wf_sb", [96, D], BF16)
        w1_sb = sb("w1_sb", [128, 8], F32)
        w2_sb = sb("w2_sb", [128, 8], F32)
        b1_sb = sb("b1_sb", [128, 8], F32)
        b2_sb = sb("b2_sb", [128, 1], F32)
        cos_r = [sb(f"cos_r{i}", [128, 32], F16) for i in range(4)]
        sin_r = [sb(f"sin_r{i}", [128, 32], F16) for i in range(4)]
        ssq_r = [sb(f"ssqc{i}", [128, 1], F32) for i in range(2)]
        m_r = [sb(f"mc{i}", [128, 1], F32) for i in range(2)]
        msq_r = [sb(f"msqc{i}", [128, 1], F32) for i in range(2)]
        var_r = [sb(f"varc{i}", [128, 1], F32) for i in range(2)]
        rstd_r = [sb(f"rstdc{i}", [128, 1], F32) for i in range(2)]
        bcol_r = [sb(f"bcolc{i}", [128, 1], F32) for i in range(2)]
        epscol = sb("epscol", [128, 1], F32)
        epssm = sb("epssm", [128, 1], F32)
        statcols = lambda r: (m_r[r], ssq_r[r], msq_r[r], var_r[r], rstd_r[r], bcol_r[r])
        st2 = [[sb(f"st2_{i}_{k}", [128, 1], F32) for i in range(6)] for k in range(2)]
        statcols2 = lambda r: tuple(st2[r])

        p = Plan()
        p.add("scalar", lambda e: e.add_instruction(mybir.InstLoadActFuncSet(
            name=nc.get_next_instruction_name(), act_func_set_id=ACT_SET_LN_EXP,
            ins=[], outs=[])))
        K.run(p, "vector", lambda e: e.memset(ones32[:, :], 0.0))
        K.run(p, "vector", lambda e: e.memset(ones32[0:1, :], 1.0), inc="dve")
        K.run(p, "vector", lambda e: e.memset(epscol[:, :], EPS_LN), inc="dve")
        K.run(p, "vector", lambda e: e.memset(epssm[:, :], EPS_SM), inc="dve")
        K.dma(p, "sync", cmask[:, :], cm_in, "sp")
        K.dma(p, "sync", w1_sb[:, :], w1_in, "sp")
        K.dma(p, "sync", w2_sb[:, :], w2_in, "sp")
        K.dma(p, "sync", b2_sb[:, :], b2_in, "sp")
        with contextlib.ExitStack() as pctx:
            wtmp_f = pctx.enter_context(nc.sbuf_tensor("wtmp_f", [128, 2 * D], F32))
            wtmp_b = pctx.enter_context(nc.sbuf_tensor("wtmp_b", [128, 2 * D], BF16))
            b1tmp = pctx.enter_context(nc.sbuf_tensor("b1tmp", [64, 16], F32))
            btmp = pctx.enter_context(nc.sbuf_tensor("btmp", [64, 1], F32))

            def cast_rows(src, dst, rows, cols):
                r0 = 0
                while r0 < rows:
                    r = min(128, rows - r0)
                    v = K.dma(p, "sync", wtmp_f[0:r, 0:cols], src[r0:r0 + r, :], "sp")
                    K.wait(p, "vector", "sp", v)
                    cv = K.run(p, "vector",
                               lambda e, r_=r, c_=cols: e.tensor_copy(wtmp_b[0:r_, 0:c_], wtmp_f[0:r_, 0:c_]),
                               inc="dve")
                    K.wait(p, "sync", "dve", cv)
                    K.dma(p, "sync", dst[r0:r0 + r, :], wtmp_b[0:r, 0:cols], "sp")
                    r0 += r

            cast_rows(wqk_in, wqk_bf, D + 32, 2 * D)
            cast_rows(wv_in, wv_bf, D + 32, D)
            cast_rows(wo_in, wo_bf, D + 32, D)
            win_src = bass.AP(win_in.tensor, win_in.offset,
                              [[128, 128], [128 * 128, NCH], [1, 128]])
            win_dst = bass.AP(win_sb[:, :].tensor, win_sb[:, :].offset,
                              [win_sb[:, :].ap[0], [128, NCH], [1, 128]])
            K.dma(p, "gpsimd", win_dst, win_src, "gp")
            K.dma(p, "gpsimd", winA_sb[:, :], win_in[D:D + 32, :], "gp")
            K.dma(p, "gpsimd", wf_sb[:, :], wf_in, "gp")
            K.dma(p, "sync", b1_sb[:, :], b1_in, "sp")
            prep_sp_done = K.cnt["sp"]
            prep_gp_done = K.cnt["gp"]
            K.flush(p)
        osl_last_sp = 0
        rload_last_gp = 0
        p = Plan()

        for b in range(BPC):
            row0 = b * S
            with contextlib.ExitStack() as sctx:
                ssb = lambda name, shape, dt: sctx.enter_context(nc.sbuf_tensor(f"{name}_{b}", shape, dt))
                qT = ssb("qT", [128, NCH * S], BF16)
                kT = ssb("kT", [128, NCH * S], BF16)
                vt = [ssb(f"vt{i}", [128, H * 65], BF16) for i in range(TT)]

                with contextlib.ExitStack() as p1ctx:
                    psb = lambda name, shape, dt: p1ctx.enter_context(nc.sbuf_tensor(f"{name}_{b}", shape, dt))
                    xTr = psb("xTr", [128, 4 * NCH * 128], BF16)
                    wqkv_sb = psb("wqkv_sb", [128, NCH * 3 * D], BF16)
                    wqkvA_sb = psb("wqkvA_sb", [32, 3 * D], BF16)
                    xb_r = [psb(f"xb{i}", [128, D], BF16) for i in range(5)]
                    qraw_r = [psb(f"qraw{i}", [128, D], BF16) for i in range(3)]
                    kraw_r = [psb(f"kraw{i}", [128, D], BF16) for i in range(3)]
                    rt = [psb(f"rt{i}", [128, 512], BF16) for i in range(4)]
                    rtk = [psb(f"rtk{i}", [128, 512], BF16) for i in range(4)]
                    rq_r = [psb(f"rq{i}", [128, D], BF16) for i in range(3)]
                    rk_r = [psb(f"rk{i}", [128, D], BF16) for i in range(3)]
                    psQ = p1ctx.enter_context(nc.psum_tensor(f"psQ_{b}", [128, 1024], F32))
                    psK = p1ctx.enter_context(nc.psum_tensor(f"psK_{b}", [128, 1024], F32))
                    psV = p1ctx.enter_context(nc.psum_tensor(f"psV_{b}", [128, 1024], F32))

                    K.soft_barrier(p)
                    if b == 0:
                        K.wait(p, "sync", "sp", prep_sp_done)
                        K.wait(p, "sync", "gp", prep_gp_done)
                        K.wait(p, "scalar", "sp", prep_sp_done)
                    K.dma(p, "sync", wqkvA_sb[:, 0:2 * D], wqk_bf[D:D + 32, :], "sp")
                    wldA = K.dma(p, "sync", wqkvA_sb[:, 2 * D:3 * D], wv_bf[D:D + 32, :], "sp")
                    wld_qk = [0] * NCH
                    wld_v = [0] * NCH
                    for c in range(NCH):
                        wld_qk[c] = K.dma(p, "sync", wqkv_sb[:, c * 3 * D:c * 3 * D + 2 * D],
                                          wqk_bf[c * 128:(c + 1) * 128, :], "sp")
                        wld_v[c] = K.dma(p, "scalar", wqkv_sb[:, c * 3 * D + 2 * D:(c + 1) * 3 * D],
                                         wv_bf[c * 128:(c + 1) * 128, :], "fld")
                    for t in range(TT):
                        K.run(p, "vector", lambda e, t_=t: e.memset(
                            strided_ap(vt[t_], 64, 65, H, 1), 1.0),
                            inc="dve" if t == TT - 1 else None)
                    vt_ms = K.cnt["dve"]

                    x_tp = [0] * TT
                    qk_tp = [0] * TT
                    qt_tp = [0] * TT
                    mulk_done = [0] * TT
                    ev_q = [0] * TT
                    ev_k = [0] * TT
                    vev = [0] * TT
                    tile_pe = [0] * TT
                    mul_done = [0] * TT
                    mul_done_gp = [0] * TT
                    rope_done = [0] * TT
                    cast_hist = [0] * TT

                    def emit_xload(tl):
                        if tl >= 5:
                            K.wait(p, "gpsimd", "tp", x_tp[tl - 5])
                        cast_hist[tl] = K.dma(p, "gpsimd", xb_r[tl % 5][:, :],
                                              x_in[row0 + tl * 128: row0 + (tl + 1) * 128, :], "gp")

                    def emit_xT(tx):
                        K.wait(p, "sync", "gp", cast_hist[tx])
                        if tx >= 4:
                            K.wait(p, "sync", "pe", tile_pe[tx - 4])
                        xT_view = bass.AP(xTr[:, :].tensor, xTr[:, :].offset + (tx % 4) * 1024,
                                          [xTr[:, :].ap[0], [128, NCH], [1, 128]])
                        K.dmat(p, "sync", xT_view, xb_r[tx % 5][:, :], "tp")
                        x_tp[tx] = K.cnt["tp"]

                    def emit_kT(tk):
                        K.wait(p, "sync", "dve", mulk_done[tk])
                        kT_view = bass.AP(kT[:, :].tensor, kT[:, :].offset + tk * 128,
                                          [kT[:, :].ap[0], [S, NCH], [1, 128]])
                        K.dmat(p, "sync", kT_view, rk_r[tk % 3][:, :], "tp")
                        qk_tp[tk] = K.cnt["tp"]

                    for tl in range(4):
                        emit_xload(tl)
                    emit_xT(0)
                    emit_xT(1)
                    for t in range(TT):
                        r = t % 3
                        u = t % 4
                        if t + 4 < TT:
                            emit_xload(t + 4)
                        if t + 2 < TT:
                            emit_xT(t + 2)
                        if t >= 1:
                            emit_kT(t - 1)
                        if t >= 4:
                            K.wait(p, "gpsimd", "dve", mul_done[t - 4])
                            K.wait(p, "gpsimd", "pool", mul_done_gp[t - 4])
                        K.dma(p, "gpsimd", cos_r[t % 4][:, :], cos_in[t * 128:(t + 1) * 128, :], "cs")
                        cs_v = K.dma(p, "gpsimd", sin_r[t % 4][:, :], sin_in[t * 128:(t + 1) * 128, :], "cs")

                        K.wait(p, "tensor", "tp", x_tp[t])
                        pe_q = pe_k = pe_v = 0
                        for which, ps in ((0, psQ), (1, psK), (2, psV)):
                            if which < 2:
                                if t >= 1:
                                    K.wait(p, "tensor", "act", (ev_q if which == 0 else ev_k)[t - 1])
                            else:
                                if t >= 1:
                                    K.wait(p, "tensor", "dve", vev[t - 1])
                            for piece in range(2):
                                col0 = which * 1024 + piece * 512
                                for c in range(NCH):
                                    if t == 0:
                                        if which < 2:
                                            K.wait(p, "tensor", "sp", wld_qk[c])
                                        else:
                                            K.wait(p, "tensor", "fld", wld_v[c])
                                    def mmf(e, c_=c, t_=t, ps_=ps, col0_=col0, piece_=piece, u_=u):
                                        return e.matmul(
                                            ps_[:, piece_ * 512:(piece_ + 1) * 512],
                                            xTr[:, u_ * 1024 + c_ * 128:u_ * 1024 + (c_ + 1) * 128],
                                            wqkv_sb[:, c_ * 3 * D + col0_: c_ * 3 * D + col0_ + 512],
                                            start=(c_ == 0), stop=False)
                                    K.run(p, "tensor", mmf)
                                if t == 0:
                                    K.wait(p, "tensor", "sp", wldA)
                                def mmA(e, ps_=ps, col0_=col0, piece_=piece):
                                    return e.matmul(ps_[:, piece_ * 512:(piece_ + 1) * 512],
                                                    ones32[:, 0:128],
                                                    wqkvA_sb[:, col0_:col0_ + 512],
                                                    start=False, stop=True)
                                v_ = K.run(p, "tensor", mmA, inc="pe")
                            if which == 0:
                                pe_q = v_
                            elif which == 1:
                                pe_k = v_
                            else:
                                pe_v = v_
                        tile_pe[t] = pe_v

                        if t >= 3:
                            K.wait(p, "scalar", "dve", mul_done[t - 3])
                            K.wait(p, "scalar", "pool", mul_done_gp[t - 3])
                        K.wait(p, "scalar", "pe", pe_q)
                        ev_q[t] = K.run(p, "scalar",
                                        lambda e, r_=r: e.copy(qraw_r[r_][:, :], psQ[:, :]), inc="act")
                        K.wait(p, "scalar", "pe", pe_k)
                        ev_k[t] = K.run(p, "scalar",
                                        lambda e, r_=r: e.copy(kraw_r[r_][:, :], psK[:, :]), inc="act")
                        K.wait(p, "vector", "pe", pe_v)
                        vev[t] = K.run(p, "vector", lambda e, t_=t: e.tensor_copy(
                            strided_ap(vt[t_], 0, 65, H, 64), psV[:, :]), inc="dve")

                        cosrep = rep_ap(cos_r[t % 4], 16, 32)
                        sinrep = rep_ap(sin_r[t % 4], 16, 32)
                        K.wait(p, "vector", "act", ev_q[t])
                        K.wait(p, "vector", "cs", cs_v)
                        if t >= 3:
                            K.wait(p, "vector", "tq", qt_tp[t - 3])
                        qr = qraw_r[r]
                        K.run(p, "vector", lambda e, raw_=qr, c_=cosrep: e.tensor_tensor(
                            rt[0][:, :], raw_[:, 0:512], c_, ALU.mult))
                        K.run(p, "vector", lambda e, raw_=qr, s_=sinrep: e.tensor_tensor(
                            rt[1][:, :], raw_[:, 512:1024], s_, ALU.mult))
                        K.run(p, "vector", lambda e, raw_=qr, s_=sinrep: e.tensor_tensor(
                            rt[2][:, :], raw_[:, 0:512], s_, ALU.mult))
                        d1 = K.run(p, "vector", lambda e, raw_=qr, c_=cosrep: e.tensor_tensor(
                            rt[3][:, :], raw_[:, 512:1024], c_, ALU.mult), inc="dve")
                        K.wait(p, "vector", "dve", d1)
                        re_q = strided_ap(rq_r[r], 0, 64, 16, 32)
                        ro_q = strided_ap(rq_r[r], 32, 64, 16, 32)
                        K.run(p, "vector", lambda e, a=re_q: e.tensor_sub(a, rt[0][:, :], rt[1][:, :]))
                        mul_done[t] = K.run(p, "vector", lambda e, a=ro_q: e.tensor_add(
                            a, rt[2][:, :], rt[3][:, :]), inc="dve")

                        def emit_qT(tq_):
                            K.wait(p, "scalar", "dve", mul_done[tq_])
                            qT_view = bass.AP(qT[:, :].tensor, qT[:, :].offset + tq_ * 128,
                                              [qT[:, :].ap[0], [S, NCH], [1, 128]])
                            K.dmat(p, "scalar", qT_view, rq_r[tq_ % 3][:, :], "tq")
                            qt_tp[tq_] = K.cnt["tq"]

                        if t >= 1:
                            emit_qT(t - 1)
                        K.wait(p, "gpsimd", "act", ev_k[t])
                        K.wait(p, "gpsimd", "cs", cs_v)
                        kr = kraw_r[r]
                        K.run(p, "gpsimd", lambda e, raw_=kr, c_=cosrep: e.tensor_tensor(
                            rtk[0][:, :], raw_[:, 0:512], c_, ALU.mult))
                        K.run(p, "gpsimd", lambda e, raw_=kr, s_=sinrep: e.tensor_tensor(
                            rtk[1][:, :], raw_[:, 512:1024], s_, ALU.mult))
                        g1 = K.run(p, "gpsimd", lambda e, raw_=kr, s_=sinrep: e.tensor_tensor(
                            rtk[2][:, :], raw_[:, 0:512], s_, ALU.mult), inc="pool")
                        mul_done_gp[t] = g1
                        K.wait(p, "vector", "act", ev_k[t])
                        d2 = K.run(p, "vector", lambda e, raw_=kr, c_=cosrep: e.tensor_tensor(
                            rtk[3][:, :], raw_[:, 512:1024], c_, ALU.mult), inc="dve")
                        if t >= 3:
                            K.wait(p, "vector", "tp", qk_tp[t - 3])
                        K.wait(p, "vector", "pool", g1)
                        K.wait(p, "vector", "dve", d2)
                        re_k = strided_ap(rk_r[r], 0, 64, 16, 32)
                        ro_k = strided_ap(rk_r[r], 32, 64, 16, 32)
                        K.run(p, "vector", lambda e, a=re_k: e.tensor_sub(a, rtk[0][:, :], rtk[1][:, :]))
                        km = K.run(p, "vector", lambda e, a=ro_k: e.tensor_add(
                            a, rtk[2][:, :], rtk[3][:, :]), inc="dve")
                        mulk_done[t] = km
                        mul_done[t] = max(mul_done[t], km)
                    emit_kT(TT - 1)
                    emit_qT(TT - 1)
                    qk_tp_done = K.cnt["tp"]
                    qt_tp_done = K.cnt["tq"]
                    vev_done = vev[TT - 1]

                with contextlib.ExitStack() as p2ctx:
                    p2sb = lambda name, shape, dt: p2ctx.enter_context(nc.sbuf_tensor(f"{name}_{b}", shape, dt))
                    stagA_r = [None, None]
                    stagB_r = [None, None]
                    Rrec_r = [None, None]
                    oA_r = [None, None]
                    oB_r = [None, None]
                    for i in (0, 1):
                        stagA_r[i] = p2sb(f"stagA{i}", [65, S], F32)
                        stagB_r[i] = p2sb(f"stagB{i}", [65, S], F32)
                        Rrec_r[i] = p2sb(f"Rrec{i}", [64, 2 * S], F32)
                        oA_r[i] = p2sb(f"oA_r{i}", [64, S], BF16)
                        oB_r[i] = p2sb(f"oB_r{i}", [64, S], BF16)
                    pk_r2 = [p2sb(f"pk_r{i}", [128, 32], F32) for i in range(2)]
                    pk_o2 = [p2sb(f"pk_o{i}", [128, 32], F32) for i in range(2)]
                    ex_r = [p2sb(f"ex{i}", [128, 1024], BF16) for i in range(3)]
                    psS_r = [p2ctx.enter_context(nc.psum_tensor(f"psSr{i}_{b}", [128, 1024], F32)) for i in range(3)]
                    psOA = p2ctx.enter_context(nc.psum_tensor(f"psOA_{b}", [65, 512], F32))
                    psOB = p2ctx.enter_context(nc.psum_tensor(f"psOB_{b}", [65, 512], F32))

                    ostore = {}
                    norm_pool_hist = {}
                    recip_hist = {}
                    sums_hist = {}
                    pkl_hist = {}
                    ev_done = 0

                    def emit_recip_load(gg):
                        pk_r = pk_r2[gg % 2]
                        K.wait(p, "gpsimd", "sums", sums_hist[gg])
                        if gg >= 2:
                            K.wait(p, "gpsimd", "pks", K.cnt["pks"] - 16)
                            K.wait(p, "gpsimd", "act", recip_hist.get(gg - 2, 0))
                            K.wait(p, "gpsimd", "pool", norm_pool_hist.get(gg - 2, 0))
                        base = sums_d.offset + (b * NHG + gg) * 2 * S
                        pkl_hist[gg] = K.dma(p, "gpsimd", pk_r[:, :],
                                             bass.AP(sums_d.tensor, base,
                                                     [[16, 128], [S, 2], [1, 16]]), "pkl")

                    def emit_recip_compute(gg):
                        nonlocal rload_last_gp
                        stagA, stagB = stagA_r[gg % 2], stagB_r[gg % 2]
                        Rrec = Rrec_r[gg % 2]
                        pk_r, pk_o = pk_r2[gg % 2], pk_o2[gg % 2]
                        K.wait(p, "scalar", "pkl", pkl_hist[gg])
                        K.run(p, "scalar", lambda e, a=pk_r: e.activation(
                            a[:, :], a[:, :], AF.Ln, bias=epssm[:, :]))
                        rc = K.run(p, "scalar", lambda e, a=pk_r, o=pk_o: e.activation(
                            o[:, :], a[:, :], AF.Exp, scale=-1.0), inc="act")
                        recip_hist[gg] = rc
                        K.wait(p, "gpsimd", "act", rc)
                        rbase = rrec_d.offset + (b * NHG + gg) * 2 * S
                        pks_v = K.dma(p, "gpsimd",
                                      bass.AP(rrec_d.tensor, rbase,
                                              [[16, 128], [S, 2], [1, 16]]),
                                      pk_o[:, :], "pks")
                        K.wait(p, "gpsimd", "pks", pks_v)
                        rload = K.dma(p, "gpsimd", Rrec[:, :],
                                      bass.AP(rrec_d.tensor, rbase,
                                              [[0, 64], [S, 2], [1, S]]), "rld")
                        rload_last_gp = rload
                        orrA, orrB = oA_r[gg % 2], oB_r[gg % 2]
                        K.wait(p, "gpsimd", "rld", rload)
                        if gg >= 2:
                            K.wait(p, "gpsimd", *ostore[gg - 2])
                        for j4 in range(4):
                            c4 = j4 * 512
                            K.run(p, "gpsimd", lambda e, o_=orrA, sA=stagA, R=Rrec, c_=c4:
                                  e.tensor_mul(o_[:, c_:c_ + 512], sA[0:64, c_:c_ + 512],
                                               R[:, c_:c_ + 512]))
                            onorm = K.run(p, "gpsimd", lambda e, o_=orrB, sB=stagB, R=Rrec, c_=c4: e.tensor_mul(
                                o_[:, c_:c_ + 512], sB[0:64, c_:c_ + 512], R[:, S + c_:S + c_ + 512]),
                                inc="pool" if j4 == 3 else None)
                        norm_pool_hist[gg] = onorm
                        ofam = "os2" if gg < NHG - 1 else "ost"
                        if gg == 0:
                            K.wait(p, "gpsimd", "sp", osl_last_sp)
                        K.dma(p, "gpsimd", oT_d[b, gg][0:64, :], orrA[:, :], ofam)
                        ostore[gg] = (ofam, K.dma(p, "gpsimd", oT_d[b, gg][64:128, :], orrB[:, :], ofam))


                    items = []
                    for g_ in range(NHG):
                        for qb_ in range(4):
                            nkt_ = (qb_ * 512 + 512) // 128
                            for kt_ in range(nkt_):
                                items.append((g_, qb_, kt_, kt_ == nkt_ - 1))
                    NIT = len(items)
                    exp_done_f = {}
                    av_done_f = {}
                    av_ready_f = {}
                    sums_st = 0

                    def emit_av_f(j):
                        nonlocal ev_done, sums_st
                        g_, qb_, kt_, last_ = items[j]
                        q0_ = qb_ * 512
                        coff = max(0, kt_ * 128 - q0_)
                        need_dve, need_act = av_ready_f[j]
                        if need_dve:
                            K.wait(p, "tensor", "dve", need_dve)
                        else:
                            K.wait(p, "tensor", "act", need_act)
                        if kt_ == 0:
                            K.wait(p, "tensor", "dve", ev_done)
                        ex = ex_r[j % 3]
                        w_ = 512 - coff
                        for i_, (head, psO) in enumerate(((0, psOA), (1, psOB))):
                            def avf(e, head_=head, psO_=psO, ex_=ex, c0_=coff, w__=w_,
                                    kt2_=kt_, g2_=g_):
                                return e.matmul(
                                    psO_[:, c0_:c0_ + w__],
                                    vt[kt2_][:, (2 * g2_ + head_) * 65:(2 * g2_ + head_) * 65 + 65],
                                    ex_[:, head_ * 512 + c0_:head_ * 512 + c0_ + w__],
                                    start=(kt2_ == 0), stop=False)
                            K.run(p, "tensor", avf, inc="pe" if i_ == 1 else None)
                        av_done_f[j] = K.cnt["pe"]
                        if last_:
                            stagA = stagA_r[g_ % 2]
                            stagB = stagB_r[g_ % 2]
                            K.wait(p, "vector", "pe", av_done_f[j])
                            if qb_ == 0 and g_ >= 2:
                                K.wait(p, "vector", "pool", norm_pool_hist.get(g_ - 2, 0))
                            K.run(p, "vector", lambda e, q0__=q0_, sA=stagA: e.tensor_copy(
                                sA[:, q0__:q0__ + 512], psOA[:, :]))
                            ev_done = K.run(p, "vector", lambda e, q0__=q0_, sB=stagB: e.tensor_copy(
                                sB[:, q0__:q0__ + 512], psOB[:, :]), inc="dve")
                            K.wait(p, "gpsimd", "dve", ev_done)
                            if g_ == 0 and qb_ == 0:
                                K.wait(p, "gpsimd", "rld", rload_last_gp)
                            K.dma(p, "gpsimd", sums_d[b, g_, 0, q0_:q0_ + 512],
                                  stagA[64:65, q0_:q0_ + 512], "sums")
                            sums_st = K.dma(p, "gpsimd", sums_d[b, g_, 1, q0_:q0_ + 512],
                                            stagB[64:65, q0_:q0_ + 512], "sums")
                            if qb_ == 3:
                                sums_hist[g_] = sums_st

                    for j, (g, qb, kt, last) in enumerate(items):
                        q0 = qb * 512
                        coff = max(0, kt * 128 - q0)
                        if kt == 0:
                            if qb == 1 and g >= 1:
                                emit_recip_load(g - 1)
                            if qb == 2 and g >= 1:
                                emit_recip_compute(g - 1)
                            if qb == 0 and g == 0:
                                K.wait(p, "tensor", "dve", K.cnt["dve"])
                                K.wait(p, "tensor", "act", K.cnt["act"])
                        if g == 0:
                            K.wait(p, "tensor", "tp", qk_tp[kt])
                            K.wait(p, "tensor", "tq", qt_tp[min(4 * qb + 3, TT - 1)])
                        if j >= 3:
                            K.wait(p, "tensor", "act", exp_done_f[j - 3])
                        w_ = 512 - coff
                        for head, base, tpos in ((0, 0, (0, 0)), (1, 64, (64, 0))):
                            def mmf(e, head_=head, base_=base, tpos_=tpos,
                                    c0_=coff, w__=w_, kt_=kt, q0_=q0, g_=g, j_=j):
                                return e.matmul(
                                    psS_r[j_ % 3][:, head_ * 512 + c0_:head_ * 512 + c0_ + w__],
                                    kT[base_:base_ + 64, g_ * S + kt_ * 128:g_ * S + (kt_ + 1) * 128],
                                    qT[base_:base_ + 64, g_ * S + q0_ + c0_:g_ * S + q0_ + c0_ + w__],
                                    start=True, stop=True, tile_position=tpos_)
                            K.run(p, "tensor", mmf, inc="pe")
                        pe_sc = K.cnt["pe"]
                        if j >= 2:
                            emit_av_f(j - 2)
                        K.wait(p, "scalar", "pe", pe_sc)
                        if j >= 3:
                            K.wait(p, "scalar", "pe", av_done_f.get(j - 3, 0))
                        exd = ex_r[j % 3]
                        def exf(e, ex_=exd, coff_=coff, ps_=psS_r[j % 3]):
                            pap = ps_[:, :].ap
                            xap = ex_[:, :].ap
                            src = bass.AP(ps_[:, :].tensor, ps_[:, :].offset + coff_,
                                          [pap[0], [512, 2], [1, 512 - coff_]])
                            dst = bass.AP(ex_[:, :].tensor, ex_[:, :].offset + coff_,
                                          [xap[0], [512, 2], [1, 512 - coff_]])
                            return e.activation(dst, src, AF.Exp, scale=0.125)
                        K.run(p, "scalar", exf, inc="act")
                        exp_done_f[j] = K.cnt["act"]
                        if kt * 128 >= q0:
                            K.wait(p, "vector", "act", exp_done_f[j])
                            def mkf(e, ex_=exd, c_=coff):
                                xap = ex_[:, :].ap
                                v2 = bass.AP(ex_[:, :].tensor, ex_[:, :].offset + c_,
                                             [xap[0], [512, 2], [1, 128]])
                                cm2 = bass.AP(cmask[:, :].tensor, cmask[:, :].offset,
                                              [cmask[:, :].ap[0], [0, 2], [1, 128]])
                                return e.tensor_tensor(v2, v2, cm2, ALU.mult)
                            mask_cnt = K.run(p, "vector", mkf, inc="dve")
                            av_ready_f[j] = (mask_cnt, None)
                        else:
                            av_ready_f[j] = (None, exp_done_f[j])
                    emit_av_f(NIT - 2)
                    emit_av_f(NIT - 1)
                    emit_recip_load(NHG - 1)
                    emit_recip_compute(NHG - 1)
                    oT_stores_done = K.cnt["ost"]

                with contextlib.ExitStack() as p3ctx:
                    p3 = lambda name, shape, dt: p3ctx.enter_context(nc.sbuf_tensor(f"{name}_{b}", shape, dt))
                    wo_sb = p3("wo_sb", [128, NCH * D], BF16)
                    woA_sb = p3("woA_sb", [32, D], BF16)
                    osl_r = [p3(f"osl{i}", [128, NCH * 128], BF16) for i in range(3)]
                    xf3_r = [p3(f"xf3{i}", [128, D], F32) for i in range(3)]
                    y_r = [p3(f"y{i}", [128, D], F32) for i in range(2)]
                    ln1f = [p3(f"ln1f{i}", [128, D], F32) for i in range(4)]
                    ln1b_r = [p3(f"ln1b{i}", [128, D], BF16) for i in range(2)]
                    ln1T = p3("ln1T", [128, NCH * 512], BF16)
                    h_r = [p3(f"h{i}", [128, 512], BF16) for i in range(2)]
                    gA = p3("gA", [96, 512], BF16)
                    gB = p3("gB", [128, 512], F32)
                    gC = p3("gC", [128, 512], F32)
                    godd = p3("godd", [64, 512], F32)
                    yo_r = [p3(f"yo{i}", [128, D], F32) for i in range(2)]
                    y2_r = [p3(f"y2_{i}", [128, D], F32) for i in range(2)]
                    sq_scr = p3("sq_scr", [128, D], BF16)
                    psP = p3ctx.enter_context(nc.psum_tensor(f"psP_{b}", [128, 1024], F32))
                    psS2 = p3ctx.enter_context(nc.psum_tensor(f"psS2_{b}", [128, 512], F32))
                    psF_r = [p3ctx.enter_context(nc.psum_tensor(f"psF{i}_{b}", [128, 1024], F32))
                             for i in range(2)]

                    K.wait(p, "sync", "pool", norm_pool_hist[NHG - 2])
                    K.wait(p, "scalar", "pool", norm_pool_hist[NHG - 2])
                    K.wait(p, "vector", "pool", K.cnt["pool"])
                    woldA = K.dma(p, "sync", woA_sb[:, :], wo_bf[D:D + 32, :], "sp")
                    wold_c = [0] * NCH
                    for c in range(NCH):
                        if c % 2 == 0:
                            wold_c[c] = ("sp", K.dma(p, "sync", wo_sb[:, c * D:(c + 1) * D],
                                                     wo_bf[c * 128:(c + 1) * 128, :], "sp"))
                        else:
                            wold_c[c] = ("fld", K.dma(p, "scalar", wo_sb[:, c * D:(c + 1) * D],
                                                      wo_bf[c * 128:(c + 1) * 128, :], "fld"))
                    K.run(p, "vector", lambda e: e.memset(gA[64:96, :], 0.0))
                    K.run(p, "vector", lambda e: e.memset(gA[64:65, :], 1.0), inc="dve")

                    resid_done = {}
                    resid2_done = {}
                    opmm_done = {}
                    store_done = {}
                    ln1_tp = {}
                    sq1 = {}
                    sq2 = {}
                    norm1_pool = {}
                    norm2_pool = {}
                    oldvs = {}
                    xldvs = {}
                    ffnout_done = 0
                    ffn_sproj_done = 0
                    gchain_done = 0
                    last_fold = 0

                    def prefetch_slice(s_):
                        rr = s_ % 3
                        if s_ >= 3:
                            K.wait(p, "sync", "pe", opmm_done.get(s_ - 3, 0))
                        ol = 0
                        for c in range(NCH):
                            if s_ <= 3:
                                fam, cnt = ostore.get(c, ("ost", 0))
                                K.wait(p, "sync", fam, cnt)
                            ol = K.dma(p, "sync", osl_r[rr][:, c * 128:(c + 1) * 128],
                                       oT_d[b, c][:, s_ * 128:(s_ + 1) * 128], "sp")
                        oldvs[s_] = ol
                        if s_ >= 3:
                            K.wait(p, "scalar", "dve", resid_done.get(s_ - 3, 0))
                        if s_ <= 3:
                            K.wait(p, "scalar", *ostore.get(NHG - 2, ("ost", 0)))
                        xldvs[s_] = K.dma(p, "scalar", xf3_r[rr][:, :],
                                          x_in[row0 + s_ * 128: row0 + (s_ + 1) * 128, :], "fld")

                    def finish_ln1(s_):
                        si_ = s_ % 4
                        r2_ = s_ % 2
                        norm1_pool[s_] = _ln_partB(
                            K, p, y_r[r2_], statcols(r2_), sq1[s_],
                            outs=[ln1f[si_][:, :], ln1b_r[r2_][:, :]], epscol=epscol,
                            pool_waits=[("dve", resid2_done.get(s_ - 4, 0)),
                                        ("tq", ln1_tp.get(s_ - 2, 0))])
                        K.wait(p, "scalar", "pool", norm1_pool[s_])
                        K.wait(p, "scalar", "pe", ffn_sproj_done)
                        lnT_view = bass.AP(ln1T[:, :].tensor, ln1T[:, :].offset + si_ * 128,
                                           [ln1T[:, :].ap[0], [512, NCH], [1, 128]])
                        K.dmat(p, "scalar", lnT_view, ln1b_r[r2_][:, :], "tq")
                        ln1_tp[s_] = K.cnt["tq"]

                    def finish_ln2(s_):
                        si_ = s_ % 4
                        r2_ = s_ % 2
                        norm2_pool[s_] = _ln_partB(
                            K, p, y2_r[r2_], statcols2(r2_), sq2[s_],
                            outs=[yo_r[r2_][:, :]], epscol=epscol,
                            pool_waits=[("st", store_done.get(s_ - 2, 0))])
                        K.wait(p, "gpsimd", "pool", norm2_pool[s_])
                        store_done[s_] = K.dma(p, "gpsimd",
                                               out_d[row0 + s_ * 128: row0 + (s_ + 1) * 128, :],
                                               yo_r[r2_][:, :], "st")

                    def emit_ffn_head(jj):
                        nonlocal ffn_sproj_done, gchain_done, last_fold
                        K.wait(p, "tensor", "tq", ln1_tp[4 * jj + 3])
                        K.wait(p, "tensor", "dve", gchain_done)
                        for c in range(NCH):
                            def mmf(e, c_=c):
                                return e.matmul(psS2[:, :], win_sb[:, c_ * 128:(c_ + 1) * 128],
                                                ln1T[:, c_ * 512:(c_ + 1) * 512],
                                                start=(c_ == 0), stop=False)
                            K.run(p, "tensor", mmf)
                        ffn_sproj_done = K.run(p, "tensor", lambda e: e.matmul(
                            psS2[:, :], winA_sb[:, :], ones32[0:32, 0:512],
                            start=False, stop=True), inc="pe")
                        gstep = {}
                        hv = {}
                        for i in range(8):
                            K.wait(p, "scalar", "pe", ffn_sproj_done)
                            if i >= 2:
                                K.wait(p, "scalar", "dve", gstep[i - 2])
                            def hf(e, i_=i, hr=h_r[i % 2]):
                                return e.activation(hr[:, :], psS2[:, :], AF.Relu,
                                                    bias=b1_sb[:, i_:i_ + 1], scale=w1_sb[:, i_:i_ + 1])
                            hv[i] = K.run(p, "scalar", hf, inc="act")
                            K.wait(p, "vector", "act", hv[i])
                            if i == 0:
                                K.wait(p, "vector", "pe", ffnout_done)
                                K.wait(p, "vector", "fld", last_fold)
                                def g0(e):
                                    return e.tensor_scalar(gB[:, :], h_r[0][:, :],
                                                           w2_sb[:, 0:1], b2_sb[:, :], ALU.mult, ALU.add)
                                gstep[0] = K.run(p, "vector", g0, inc="dve")
                            else:
                                K.wait(p, "vector", "dve", gstep[i - 1])
                                src, dst = (gB[:, :], gC[:, :]) if i % 2 == 1 else (gC[:, :], gB[:, :])
                                def gi(e, i_=i, s_=src, d_=dst, hr=h_r[i % 2]):
                                    return e.scalar_tensor_tensor(d_, hr[:, :], w2_sb[:, i_:i_ + 1],
                                                                  s_, ALU.mult, ALU.add)
                                gstep[i] = K.run(p, "vector", gi, inc="dve")
                        K.wait(p, "scalar", "dve", gstep[7])
                        fold_v = K.dma(p, "scalar", godd[:, :], gC[64:128, :], "fld")
                        last_fold = fold_v
                        K.wait(p, "vector", "fld", fold_v)
                        gchain_done = K.run(p, "vector", lambda e: e.tensor_add(
                            gA[0:64, :], gC[0:64, :], godd[:, :]), inc="dve")

                    def emit_ffn_tail(jj):
                        nonlocal ffnout_done
                        for si in range(4):
                            s = 4 * jj + si
                            r2 = s % 2
                            K.wait(p, "tensor", "dve", gchain_done)
                            if s >= 2:
                                K.wait(p, "tensor", "dve", resid2_done.get(s - 2, 0))
                            pe_f = 0
                            for piece in range(2):
                                def mmf(e, si_=si, piece_=piece, ps_=psF_r[s % 2]):
                                    return e.matmul(ps_[:, piece_ * 512:(piece_ + 1) * 512],
                                                    gA[:, si_ * 128:(si_ + 1) * 128],
                                                    wf_sb[:, piece_ * 512:piece_ * 512 + 512],
                                                    start=True, stop=True)
                                pe_f = K.run(p, "tensor", mmf, inc="pe")
                            if si == 3:
                                ffnout_done = pe_f
                            K.wait(p, "vector", "pool", norm1_pool[s])
                            K.wait(p, "vector", "pool", norm2_pool.get(s - 2, 0))
                            K.wait(p, "scalar", "pool", norm2_pool.get(s - 2, 0))
                            resid2_done[s], sq2[s] = _ln_partA(
                                K, p, psF_r[s % 2], ln1f[si], y2_r[r2], statcols2(r2), sq_scr, pe_f)
                            if si >= 1:
                                finish_ln2(s - 1)
                        finish_ln2(4 * jj + 3)

                    for s0 in range(3):
                        prefetch_slice(s0)
                    for j in range(4):
                        for si in range(4):
                            s = 4 * j + si
                            r = s % 3
                            r2 = s % 2
                            if s == 0:
                                K.wait(p, "tensor", "dve", K.cnt["dve"])
                            K.wait(p, "tensor", "sp", oldvs[s])
                            if s >= 1:
                                K.wait(p, "tensor", "dve", resid_done.get(s - 1, 0))
                            for piece in range(2):
                                for c in range(NCH):
                                    if s == 0:
                                        K.wait(p, "tensor", wold_c[c][0], wold_c[c][1])
                                    def mmf(e, c_=c, r_=r, piece_=piece):
                                        return e.matmul(psP[:, piece_ * 512:(piece_ + 1) * 512],
                                                        osl_r[r_][:, c_ * 128:(c_ + 1) * 128],
                                                        wo_sb[:, c_ * D + piece_ * 512: c_ * D + piece_ * 512 + 512],
                                                        start=(c_ == 0), stop=False)
                                    K.run(p, "tensor", mmf)
                                if s == 0:
                                    K.wait(p, "tensor", "sp", woldA)
                                def mmA(e, piece_=piece):
                                    return e.matmul(psP[:, piece_ * 512:(piece_ + 1) * 512],
                                                    ones32[:, 0:128],
                                                    woA_sb[:, piece_ * 512:piece_ * 512 + 512],
                                                    start=False, stop=True)
                                K.run(p, "tensor", mmA, inc="pe")
                            opmm_done[s] = pe_op = K.cnt["pe"]
                            K.wait(p, "vector", "fld", xldvs[s])
                            K.wait(p, "vector", "pool", norm1_pool.get(s - 2, 0))
                            K.wait(p, "scalar", "pool", norm1_pool.get(s - 2, 0))
                            resid_done[s], sq1[s] = _ln_partA(
                                K, p, psP, xf3_r[r], y_r[r2], statcols(r2), sq_scr, pe_op)
                            if s + 3 < TT:
                                prefetch_slice(s + 3)
                            if si == 1 and j >= 1:
                                emit_ffn_head(j - 1)
                                emit_ffn_tail(j - 1)
                            if si >= 1:
                                finish_ln1(s - 1)
                        finish_ln1(4 * j + 3)
                    emit_ffn_head(3)
                    emit_ffn_tail(3)

        K.wait(p, "sync", "st", K.cnt["st"])
        K.flush(p)

    return nc


_PROG = None


def _get_prog():
    global _PROG
    if _PROG is None:
        _PROG = build_program()
    return _PROG


def _pair8(w):
    return np.ascontiguousarray(np.concatenate([w[:, 0::2], w[:, 1::2]], axis=0))


def _host_arrays(L0_attn_w_qkv, L0_attn_b_qkv, L0_attn_w_out, L0_attn_b_out,
                 L0_ffn_Win, L0_ffn_bin_, L0_ffn_W1, L0_ffn_b1, L0_ffn_W2,
                 L0_ffn_b2, L0_ffn_Wout, L0_ffn_bout):
    f32 = np.float32
    wqkv = np.asarray(L0_attn_w_qkv, f32)
    bqkv = np.asarray(L0_attn_b_qkv, f32)
    hh = np.arange(H)
    pp = np.arange(32)
    e_idx = (hh[:, None] * 64 + 2 * pp[None, :]).ravel()
    o_idx = e_idx + 1
    q_perm = np.concatenate([e_idx, o_idx])
    qk_cols = np.concatenate([q_perm, q_perm + D])
    wqk = np.zeros((D + 32, 2 * D), f32)
    wqk[0:D, :] = wqkv[qk_cols, :].T
    wqk[D, :] = bqkv[qk_cols]
    wv = np.zeros((D + 32, D), f32)
    wv[0:D, :] = wqkv[2 * D:3 * D, :].T
    wv[D, :] = bqkv[2 * D:3 * D]
    wo = np.zeros((D + 32, D), f32)
    wo[0:D, :] = np.asarray(L0_attn_w_out, f32).T
    wo[D, :] = np.asarray(L0_attn_b_out, f32)
    win = np.zeros((D + 32, 128), f32)
    win[0:D, 0:64] = np.asarray(L0_ffn_Win, f32)
    win[D, 0:64] = np.asarray(L0_ffn_bin_, f32)
    win[:, 64:128] = win[:, 0:64]
    wf = np.zeros((96, D), f32)
    wf[0:64, :] = np.asarray(L0_ffn_Wout, f32)
    wf[64, :] = np.asarray(L0_ffn_bout, f32)
    ang = (np.arange(S, dtype=np.float64)[:, None]
           * (10000.0 ** (-np.arange(0, DK, 2, dtype=np.float64) / DK))[None, :])
    cosT = np.cos(ang).astype(np.float16)
    sinT = np.sin(ang).astype(np.float16)
    cmask = np.triu(np.ones((128, 128), np.float32)).astype(ml_dtypes.bfloat16)
    return {
        "wqk": np.ascontiguousarray(wqk),
        "wv": np.ascontiguousarray(wv),
        "wo": np.ascontiguousarray(wo),
        "win": np.ascontiguousarray(win),
        "wf": np.ascontiguousarray(wf),
        "w1": _pair8(np.asarray(L0_ffn_W1, f32)),
        "b1": _pair8(np.asarray(L0_ffn_b1, f32)),
        "w2": _pair8(np.asarray(L0_ffn_W2, f32)),
        "binv": np.ascontiguousarray(np.asarray(L0_ffn_bin_, f32)[:, None]),
        "b2": np.ascontiguousarray(np.concatenate(
            [np.asarray(L0_ffn_b2, f32), np.zeros(64, f32)])[:, None]),
        "cosT": cosT, "sinT": sinT, "cmask": cmask,
    }


def kernel(L0_attn_x, L0_attn_w_qkv, L0_attn_b_qkv, L0_attn_w_out, L0_attn_b_out,
           L0_ffn_Win, L0_ffn_bin_, L0_ffn_W1, L0_ffn_b1, L0_ffn_W2, L0_ffn_b2,
           L0_ffn_Wout, L0_ffn_bout):
    x = np.ascontiguousarray(np.asarray(L0_attn_x, np.float32))
    shared = _host_arrays(L0_attn_w_qkv, L0_attn_b_qkv, L0_attn_w_out, L0_attn_b_out,
                          L0_ffn_Win, L0_ffn_bin_, L0_ffn_W1, L0_ffn_b1, L0_ffn_W2,
                          L0_ffn_b2, L0_ffn_Wout, L0_ffn_bout)
    nc = _get_prog()
    in_maps = []
    for core in range(NCORES):
        m = dict(shared)
        m["x"] = np.ascontiguousarray(x[core * BPC:(core + 1) * BPC].reshape(BPC * S, D))
        in_maps.append(m)
    res = run_bass_kernel_spmd(nc, in_maps, core_ids=list(range(NCORES)))
    out = np.concatenate([r["out"].reshape(BPC, S, D) for r in res.results], axis=0)
    return out.astype(np.float32)
